# version 23
# baseline (speedup 1.0000x reference)
import numpy as np
import ml_dtypes
from contextlib import ExitStack

import concourse.bass as bass
import concourse.mybir as mybir
from concourse.bass_utils import run_bass_kernel_spmd
from concourse.alu_op_type import AluOpType as OP

F32 = mybir.dt.float32
BF16 = mybir.dt.bfloat16
FP16 = mybir.dt.float16
I32 = mybir.dt.int32
AF = mybir.ActivationFunctionType
AX = mybir.AxisListType

S = 2048
D = 2048
H = 8192
NT = S // 128
NDC = D // 128
SW = 512
NW = S // SW
NS4 = SW // 128
NJT = H // 512
NJC = H // 128
QC = 1024
NQC = H // QC

LN_EPS = 1e-5
T_EPS = 1e-5
NELEM_W = float(D * H)


class Sem:
    def __init__(self, es, nc, name):
        self.h = es.enter_context(nc.semaphore(name))
        self.n = 0

    def inc(self, instr, k=1):
        instr.then_inc(self.h, k)
        self.n += k
        return self.n


class Ctx:
    def __init__(self, nc, es):
        self.nc = nc
        self.es = es
        self.block = es.enter_context(nc.Block())
        self.all_sems = []
        self.tS = self.sem("tS")
        self.vS = self.sem("vS")
        self.aS = self.sem("aS")
        self.gS = self.sem("gS")
        self._dsems = {}

    def sem(self, name):
        s = Sem(self.es, self.nc, name)
        self.all_sems.append(s)
        return s

    def dsem(self, name):
        if name not in self._dsems:
            self._dsems[name] = self.sem(name)
        return self._dsems[name]

    def te(self, f): self.block.tensor(f)
    def ve(self, f): self.block.vector(f)
    def se(self, f): self.block.scalar(f)
    def sy(self, f): self.block.sync(f)
    def gp(self, f): self.block.gpsimd(f)

    def barrier(self):
        snap = [(s.h, s.n) for s in self.all_sems if s.n]
        for sec in (self.te, self.ve, self.se, self.sy, self.gp):
            def _(eng, snap=snap):
                for h, n in snap:
                    eng.wait_ge(h, n)
            sec(_)


def build_program():
    nc = bass.Bass("TRN2", target_bir_lowering=False, debug=False)

    x_h = nc.dram_tensor("x", [S, D], F32, kind="ExternalInput").ap()
    wq_h = nc.dram_tensor("Wq", [D, D], F32, kind="ExternalInput").ap()
    wk_h = nc.dram_tensor("Wk", [D, D], F32, kind="ExternalInput").ap()
    wv_h = nc.dram_tensor("Wv", [D, D], F32, kind="ExternalInput").ap()
    wo_h = nc.dram_tensor("Wo", [D, D], F32, kind="ExternalInput").ap()
    cfc_h = nc.dram_tensor("cfc", [D, H], F32, kind="ExternalInput").ap()
    cpj_h = nc.dram_tensor("cpj", [D, H], F32, kind="ExternalInput").ap()
    id_h = nc.dram_tensor("identin", [128, 128], BF16, kind="ExternalInput").ap()
    ones_h = nc.dram_tensor("onesin", [128, 128], F32, kind="ExternalInput").ap()
    out_h = nc.dram_tensor("out", [S, D], F32, kind="ExternalOutput").ap()

    q1d = nc.dram_tensor("q1d", [NJT, 128, NDC * 512], BF16).ap()
    q2d = nc.dram_tensor("q2d", [NDC, 128, NJC * 128], BF16).ap()
    x1d = nc.dram_tensor("x1d", [S, D], F32).ap()
    x2d = nc.dram_tensor("x2d", [S, D], F32).ap()
    qd = nc.dram_tensor("qd", [S, D], BF16).ap()
    kd = nc.dram_tensor("kd", [128, NT * 2048], BF16).ap()
    od = nc.dram_tensor("od", [128, NDC * 2048], BF16).ap()
    u_d = nc.dram_tensor("u_d", [NT, 128, H], FP16).ap()
    a_d = nc.dram_tensor("a_d", [S, H], BF16).ap()
    f_d = nc.dram_tensor("f_d", [D, S], BF16).ap()

    with ExitStack() as es0:
        C = Ctx(nc, es0)
        ec = es0.enter_context

        ident = ec(nc.sbuf_tensor("g_ident", [128, 128], BF16))
        ones = ec(nc.sbuf_tensor("g_ones", [128, 128], F32))
        svec = ec(nc.sbuf_tensor("g_svec", [128, 16], F32))
        mparts = ec(nc.sbuf_tensor("g_mparts", [128, 128], F32))
        wvec = ec(nc.sbuf_tensor("g_wvec", [128, 64], F32))
        lnv = ec(nc.sbuf_tensor("g_lnv", [128, 16], F32))
        s11 = ec(nc.sbuf_tensor("g_s11", [1, 16], F32))

        sGlob = C.dsem("sGlob")

        def _(sy):
            sy.dma_start(out=ident[:], in_=id_h).then_inc(sGlob.h, 16)
            sy.dma_start(out=ones[:], in_=ones_h).then_inc(sGlob.h, 16)
        C.sy(_)
        sGlob.n += 32

        def emit_prep():
            with ExitStack() as es:
                pec = es.enter_context
                wbuf = [pec(nc.sbuf_tensor(f"p_w{i}", [128, H], F32)) for i in range(4)]
                parts = pec(nc.sbuf_tensor("p_parts", [128, 2], F32))
                pmat = pec(nc.sbuf_tensor("p_pmat", [128, 32], F32))
                junk = pec(nc.sbuf_tensor("p_junk", [128, H], BF16))
                qtB = [pec(nc.sbuf_tensor(f"p_qt{i}", [128, H], BF16)) for i in range(2)]
                ntmp = pec(nc.sbuf_tensor("p_ntmp", [128, H], mybir.dt.int8))
                psV = pec(nc.psum_tensor("p_psV", [128, 128], F32))
                sQ1S = [C.dsem(f"sQ1S{i}") for i in range(2)]

                QMAP = {'sy': C.sy, 'se': C.se, 'gp': C.gp}
                sem_pool = {}

                def q_sem(qname, sl):
                    key = (qname, sl)
                    if key not in sem_pool:
                        sem_pool[key] = C.dsem(f"sP_{qname}{sl}")
                    return sem_pool[key]

                buf_free = {sl: None for sl in range(4)}

                def load_chunks(src, dc, sl, chunks, extra_wait=None):
                    buf = wbuf[sl]
                    wfree = buf_free[sl]
                    needs = []
                    for qn, c0, c1 in chunks:
                        sB = q_sem(qn, sl)
                        def _(eng, buf=buf, src=src, dc=dc, c0=c0, c1=c1,
                              wfree=wfree, sB=sB, extra_wait=extra_wait):
                            if wfree:
                                eng.wait_ge(*wfree)
                            if extra_wait:
                                eng.wait_ge(*extra_wait)
                            eng.dma_start(
                                out=buf[:, c0:c1],
                                in_=src[dc * 128:(dc + 1) * 128, c0:c1]
                            ).then_inc(sB.h, 16)
                        QMAP[qn](_)
                        sB.n += 16
                        needs.append((sB.h, sB.n))
                    return buf, needs

                def reduce_tile(buf, needs, wi, dc):
                    aprev = C.aS.n
                    def _(se, buf=buf, needs=tuple(needs), wi=wi, dc=dc,
                          aprev=aprev):
                        for h, v in needs:
                            se.wait_ge(h, v)
                        se.wait_ge(C.aS.h, aprev)
                        i = se.activation(out=junk[:], in_=buf[:], func=AF.Abs,
                                          bias=0.0, scale=1.0,
                                          accum_out=pmat[:, wi * 16 + dc:
                                                         wi * 16 + dc + 1])
                        C.aS.inc(i)
                    C.se(_)

                def reduce_finish(wi):
                    aneed = C.aS.n
                    def _(ve, wi=wi, aneed=aneed):
                        ve.wait_ge(C.aS.h, aneed)
                        i = ve.tensor_reduce(out=parts[:, wi:wi + 1],
                                             in_=pmat[:, wi * 16:(wi + 1) * 16],
                                             axis=AX.X, op=OP.add)
                        ve.drain()
                        C.vS.inc(i)
                    C.ve(_)

                def scal_part(wi):
                    vneed = C.vS.n
                    def _(te, vneed=vneed, wi=wi):
                        te.wait_ge(C.vS.h, vneed)
                        i = te.matmul(psV[0:1, wi:wi + 1], parts[:, wi:wi + 1],
                                      ones[:, 0:1], start=True, stop=True)
                        C.tS.inc(i)
                    C.te(_)
                    tneed = C.tS.n
                    def _(ve, tneed=tneed, wi=wi):
                        ve.wait_ge(C.tS.h, tneed)
                        i = ve.tensor_scalar(out=s11[0:1, wi:wi + 1],
                                             in0=psV[0:1, wi:wi + 1],
                                             scalar1=1.0 / NELEM_W, scalar2=T_EPS,
                                             op0=OP.mult, op1=OP.max)
                        C.vS.inc(i)
                    C.ve(_)
                    vneed = C.vS.n
                    def _(te, vneed=vneed, wi=wi):
                        te.wait_ge(C.vS.h, vneed)
                        i = te.matmul(psV[:, 2 + wi:3 + wi], ones[0:1, :],
                                      s11[0:1, wi:wi + 1], start=True, stop=True)
                        C.tS.inc(i)
                    C.te(_)
                    tneed = C.tS.n
                    def _(ve, tneed=tneed, wi=wi):
                        ve.wait_ge(C.tS.h, tneed)
                        i = ve.tensor_copy(out=svec[:, 6 + wi:7 + wi],
                                           in_=psV[:, 2 + wi:3 + wi])
                        ve.drain()
                        sb = svec[:, 6 + wi:7 + wi]
                        if wi == 0:
                            i = ve.tensor_scalar(out=svec[:, 0:1], in0=sb,
                                                 scalar1=0.5, scalar2=None, op0=OP.mult)
                            i = ve.tensor_scalar(out=svec[:, 1:2], in0=sb,
                                                 scalar1=-0.5, scalar2=None, op0=OP.mult)
                            i = ve.tensor_mul(out=wvec[:, 30:31], in0=sb, in1=sb)
                            ve.drain()
                            i = ve.reciprocal(out=wvec[:, 31:32], in_=wvec[:, 30:31])
                            ve.drain()
                            i = ve.tensor_scalar(out=svec[:, 2:3], in0=wvec[:, 31:32],
                                                 scalar1=T_EPS, scalar2=None,
                                                 op0=OP.mult)
                        else:
                            i = ve.tensor_scalar(out=svec[:, 4:5], in0=sb,
                                                 scalar1=0.5, scalar2=None, op0=OP.mult)
                            i = ve.tensor_scalar(out=svec[:, 5:6], in0=sb,
                                                 scalar1=-0.5, scalar2=None, op0=OP.mult)
                            i = ve.tensor_mul(out=wvec[:, 29:30], in0=wvec[:, 30:31],
                                              in1=sb)
                            ve.drain()
                            i = ve.tensor_scalar(out=svec[:, 3:4], in0=wvec[:, 29:30],
                                                 scalar1=0.5 / 255.0, scalar2=None,
                                                 op0=OP.mult)
                        C.vS.inc(i)
                    C.ve(_)
                    return C.vS.n

                def tern_store_q1(buf, needs, scal_ready, dc):
                    qt = qtB[dc % 2]
                    sQs = sQ1S[dc % 2]
                    st_done = sQs.n
                    vprev = C.vS.n
                    def _(ve, buf=buf, needs=tuple(needs), st_done=st_done,
                          scal_ready=scal_ready, qt=qt, sQs=sQs, vprev=vprev):
                        for h, v in needs:
                            ve.wait_ge(h, v)
                        ve.wait_ge(C.vS.h, max(scal_ready, vprev))
                        ve.wait_ge(sQs.h, st_done)
                        i = ve.tensor_scalar(out=ntmp[:], in0=buf[:],
                                             scalar1=svec[:, 1:2], scalar2=None,
                                             op0=OP.is_lt)
                        ve.drain()
                        i = ve.scalar_tensor_tensor(out=qt[:], in0=buf[:],
                                                    scalar=svec[:, 0:1], in1=ntmp[:],
                                                    op0=OP.is_gt, op1=OP.subtract)
                        C.vS.inc(i)
                    C.ve(_)
                    return C.vS.n, qt

                CH3 = (('sy', 0, 2730), ('se', 2730, 5460), ('gp', 5460, H))
                PF = 3

                def emit_1a_load(dc):
                    return load_chunks(cfc_h, dc, dc % 4, CH3,
                                       (sGlob.h, sGlob.n) if dc < PF else None)
                pend = {dc: emit_1a_load(dc) for dc in range(PF)}
                for dc in range(NDC):
                    if dc + PF < NDC:
                        pend[dc + PF] = emit_1a_load(dc + PF)
                    buf, needs = pend.pop(dc)
                    reduce_tile(buf, needs, 0, dc)
                    buf_free[dc % 4] = (C.aS.h, C.aS.n)
                reduce_finish(0)
                s1_ready = scal_part(0)

                gp_cfc = {1, 4, 7, 10, 13}
                sy_cpj = {6, 12}
                for dc in range(NDC):
                    sl = dc % 2
                    qn = 'gp' if dc in gp_cfc else 'sy'
                    buf, needs = load_chunks(cfc_h, dc, sl, ((qn, 0, H),))
                    need_v, qt = tern_store_q1(buf, needs, s1_ready, dc)
                    buf_free[sl] = (C.vS.h, need_v)
                    sQs = sQ1S[dc % 2]
                    def _(gp, dc=dc, need_v=need_v, qt=qt, sQs=sQs):
                        gp.wait_ge(C.vS.h, need_v)
                        dst = q1d[:, :, dc * 512:(dc + 1) * 512].rearrange(
                            "jt p c -> p jt c")
                        src = qt[:].rearrange("p (jt c) -> p jt c", c=512)
                        gp.dma_start(out=dst, in_=src).then_inc(sQs.h, 16)
                    C.gp(_)
                    sQs.n += 16
                    qn2 = 'sy' if dc in sy_cpj else 'se'
                    buf2, needs2 = load_chunks(cpj_h, dc, 2 + sl, ((qn2, 0, H),))
                    reduce_tile(buf2, needs2, 1, dc)
                    buf_free[2 + sl] = (C.aS.h, C.aS.n)
                reduce_finish(1)
                s2_ready = scal_part(1)

                C.barrier()

        def ln_tile_stats(t, src_ap, xr, sXR, ybf, sq_scr):
            vfree = C.vS.n
            aprev = C.aS.n
            def _(se, t=t, vfree=vfree, aprev=aprev):
                se.wait_ge(C.vS.h, vfree)
                se.wait_ge(C.aS.h, aprev)
                se.dma_start(out=xr[:], in_=src_ap[t * 128:(t + 1) * 128, :]
                             ).then_inc(sXR.h, 16)
            C.se(_)
            sXR.n += 16
            need_d = sXR.n
            vprev = C.vS.n
            def _(se, need_d=need_d, vprev=vprev):
                se.wait_ge(sXR.h, need_d)
                se.wait_ge(C.vS.h, vprev)
                i = se.activation(out=sq_scr, in_=xr[:, 0:QC], func=AF.Square,
                                  bias=0.0, scale=1.0, accum_out=lnv[:, 0:1])
                se.drain()
                i = se.activation(out=sq_scr, in_=xr[:, QC:2 * QC], func=AF.Square,
                                  bias=0.0, scale=1.0, accum_out=lnv[:, 1:2])
                C.aS.inc(i)
            C.se(_)
            need_a = C.aS.n
            def _(ve, need_d=need_d, need_a=need_a):
                ve.wait_ge(sXR.h, need_d)
                ve.wait_ge(C.aS.h, need_a)
                i = ve.tensor_reduce(out=lnv[:, 2:3], in_=xr[:], axis=AX.X, op=OP.add)
                i = ve.tensor_add(out=lnv[:, 4:5], in0=lnv[:, 0:1], in1=lnv[:, 1:2])
                ve.drain()
                i = ve.tensor_scalar(out=lnv[:, 3:4], in0=lnv[:, 2:3],
                                     scalar1=1.0 / D, scalar2=None, op0=OP.mult)
                i = ve.tensor_scalar(out=lnv[:, 4:5], in0=lnv[:, 4:5],
                                     scalar1=1.0 / D, scalar2=None, op0=OP.mult)
                ve.drain()
                i = ve.tensor_mul(out=lnv[:, 5:6], in0=lnv[:, 3:4], in1=lnv[:, 3:4])
                ve.drain()
                i = ve.tensor_sub(out=lnv[:, 6:7], in0=lnv[:, 4:5], in1=lnv[:, 5:6])
                ve.drain()
                i = ve.tensor_scalar(out=lnv[:, 6:7], in0=lnv[:, 6:7],
                                     scalar1=LN_EPS, scalar2=None, op0=OP.add)
                C.vS.inc(i)
            C.ve(_)
            need_v = C.vS.n
            def _(se, need_v=need_v):
                se.wait_ge(C.vS.h, need_v)
                i = se.activation(out=lnv[:, 7:8], in_=lnv[:, 6:7], func=AF.Sqrt,
                                  bias=0.0, scale=1.0)
                C.aS.inc(i)
            C.se(_)
            need_a = C.aS.n
            def _(ve, need_a=need_a):
                ve.wait_ge(C.aS.h, need_a)
                i = ve.reciprocal(out=lnv[:, 8:9], in_=lnv[:, 7:8])
                ve.drain()
                i = ve.tensor_mul(out=lnv[:, 9:10], in0=lnv[:, 8:9], in1=lnv[:, 8:9])
                ve.drain()
                i = ve.tensor_mul(out=lnv[:, 9:10], in0=lnv[:, 9:10], in1=lnv[:, 6:7])
                ve.drain()
                i = ve.tensor_scalar(out=lnv[:, 9:10], in0=lnv[:, 9:10],
                                     scalar1=-0.5, scalar2=1.5, op0=OP.mult, op1=OP.add)
                ve.drain()
                i = ve.tensor_mul(out=lnv[:, 10:11], in0=lnv[:, 8:9], in1=lnv[:, 9:10])
                ve.drain()
                i = ve.tensor_mul(out=lnv[:, 11:12], in0=lnv[:, 3:4], in1=lnv[:, 10:11])
                ve.drain()
                i = ve.tensor_scalar(out=lnv[:, 11:12], in0=lnv[:, 11:12],
                                     scalar1=-1.0, scalar2=None, op0=OP.mult)
                C.vS.inc(i)
            C.ve(_)
            need_v = C.vS.n
            tprev = C.tS.n
            def _(se, need_v=need_v, tprev=tprev):
                se.wait_ge(C.vS.h, need_v)
                se.wait_ge(C.tS.h, tprev)
                i = se.activation(out=ybf[:], in_=xr[:], func=AF.Identity,
                                  bias=lnv[:, 11:12], scale=lnv[:, 10:11])
                C.aS.inc(i)
            C.se(_)
            return C.aS.n

        def ln_tile_transpose(need_a, ybf, yt, ytile, psT, slot):
            for grp in range(2):
                bank = psT[grp]
                vfree = C.vS.n
                def _(te, grp=grp, bank=bank, need_a=need_a, vfree=vfree):
                    te.wait_ge(C.aS.h, need_a)
                    te.wait_ge(C.vS.h, vfree)
                    for k in range(8):
                        dc = grp * 8 + k
                        i = te.transpose(bank[:, k * 128:(k + 1) * 128],
                                         ybf[:, dc * 128:(dc + 1) * 128], ident[:])
                    C.tS.inc(i)
                C.te(_)
                need_t = C.tS.n
                def _(ve, grp=grp, bank=bank, need_t=need_t, slot=slot):
                    ve.wait_ge(C.tS.h, need_t)
                    dst = yt[:].rearrange("p (dc c) -> p dc c", c=ytile)[
                        :, grp * 8:(grp + 1) * 8, slot * 128:(slot + 1) * 128]
                    i = ve.tensor_copy(out=dst, in_=bank[:].rearrange(
                        "p (k c) -> p k c", c=128))
                    C.vS.inc(i)
                C.ve(_)

        def emit_ln_tile(t, src_ap, xr, sXR, ybf, yt, ytile, sq_scr, psT, slot):
            need_a = ln_tile_stats(t, src_ap, xr, sXR, ybf, sq_scr)
            ln_tile_transpose(need_a, ybf, yt, ytile, psT, slot)

        def emit_ffn(src_ap, dst_ap, tag, with_pass3=False):
            with ExitStack() as es:
                fec = es.enter_context
                ynT = fec(nc.sbuf_tensor(f"{tag}_ynT", [128, NDC * 512], BF16))
                uR = [fec(nc.sbuf_tensor(f"{tag}_u{i}", [128, QC], FP16)) for i in range(2)]
                tP = [fec(nc.sbuf_tensor(f"{tag}_t{i}", [128, QC], F32)) for i in range(2)]
                k16 = [fec(nc.sbuf_tensor(f"{tag}_k{i}", [128, QC], mybir.dt.int16)) for i in range(2)]
                aB = [fec(nc.sbuf_tensor(f"{tag}_a{i}", [128, QC], BF16)) for i in range(4)]
                q1R = [fec(nc.sbuf_tensor(f"{tag}_q1{i}", [128, 4096], BF16)) for i in range(4)]
                fTR = [fec(nc.sbuf_tensor(f"{tag}_fT{i}", [128, 512], BF16)) for i in range(4)]
                ftok = [fec(nc.sbuf_tensor(f"{tag}_ftok{i}", [128, D], BF16)) for i in range(1)]
                xr = fec(nc.sbuf_tensor(f"{tag}_xr", [128, D], F32))
                xr2 = xr
                xo = fec(nc.sbuf_tensor(f"{tag}_xo", [128, D], F32))
                ybf = fec(nc.sbuf_tensor(f"{tag}_ybf", [128, D], BF16))
                utile = [fec(nc.sbuf_tensor(f"{tag}_ut{i}", [128, 512], FP16)) for i in range(4)]
                psA = [fec(nc.psum_tensor(f"{tag}_psA{i}", [128, 512], F32)) for i in range(4)]
                psB = [fec(nc.psum_tensor(f"{tag}_psB{i}", [128, 512], F32)) for i in range(2)]
                psT = [fec(nc.psum_tensor(f"{tag}_psT{i}", [128, 1024], BF16)) for i in range(2)]

                late = {}
                sQ1 = [C.dsem(f"sQ1_{i}") for i in range(4)]
                sQ2 = [C.dsem(f"sQ2_{i}") for i in range(4)]
                sQ2S = [C.dsem(f"sQ2S{i}") for i in range(2)]
                sUT = [C.dsem(f"sUT{i}") for i in range(4)]
                sUR = [C.dsem(f"sUR{i}") for i in range(2)]
                sAB = [C.dsem(f"sAB{i}") for i in range(4)]
                sAT = C.dsem("sAT")
                sFS = [C.dsem(f"sFS{i}") for i in range(4)]
                sFTk = [C.dsem(f"sFTk{i}") for i in range(1)]
                sXR = C.dsem("sXR")
                sXO = C.dsem("sXO")

                st = {
                    "q1_last": [0] * 4,
                    "q2_last": [0] * 4,
                    "ln_v": {},
                    "u_done": {},
                    "a_done": {},
                    "f_done": {},
                    "mm2_done": 0,
                    "u_sq": [0, 0],
                    "dve_p": [0, 0],
                }

                def wv(w, k):
                    return (w % 2) * 32 + k

                p3 = {}
                if with_pass3:
                    p3es = ExitStack()
                    p3 = {
                        "es": p3es,
                        "wb": [p3es.enter_context(
                            nc.sbuf_tensor(f"{tag}_p3w{i}", [128, 4096], F32))
                            for i in range(2)],
                        "qt": [p3es.enter_context(
                            nc.sbuf_tensor(f"{tag}_p3q{i}", [128, 4096], BF16))
                            for i in range(2)],
                        "st": [p3es.enter_context(
                            nc.sbuf_tensor(f"{tag}_p3s{i}", [128, 4096], BF16))
                            for i in range(2)],
                        "nt": p3es.enter_context(
                            nc.sbuf_tensor(f"{tag}_p3n", [128, 4096], mybir.dt.int8)),
                        "sW": [C.dsem(f"p3W{i}") for i in range(2)],
                        "tern_done": [0, 0],
                        "st_done": [0, 0],
                    }

                def emit_pass3_half(h2):
                    dc, half = divmod(h2, 2)
                    sl = h2 % 2
                    buf, sW = p3["wb"][sl], p3["sW"][sl]
                    qt, stg = p3["qt"][sl], p3["st"][sl]
                    vfree = C.vS.n
                    def _(sy, buf=buf, dc=dc, half=half, vfree=vfree, sW=sW):
                        sy.wait_ge(C.vS.h, vfree)
                        sy.dma_start(out=buf[:],
                                     in_=cpj_h[dc * 128:(dc + 1) * 128,
                                               half * 4096:(half + 1) * 4096]
                                     ).then_inc(sW.h, 16)
                    C.sy(_)
                    sW.n += 16
                    need_d = sW.n
                    tfree = p3["tern_done"][sl]
                    def _(ve, buf=buf, need_d=need_d, tfree=tfree, sW=sW, qt=qt):
                        ve.wait_ge(sW.h, need_d)
                        ve.wait_ge(C.tS.h, tfree)
                        i = ve.tensor_scalar(out=p3["nt"][:], in0=buf[:],
                                             scalar1=svec[:, 5:6], scalar2=None,
                                             op0=OP.is_lt)
                        ve.drain()
                        i = ve.scalar_tensor_tensor(out=qt[:], in0=buf[:],
                                                    scalar=svec[:, 4:5], in1=p3["nt"][:],
                                                    op0=OP.is_gt, op1=OP.subtract)
                        C.vS.inc(i)
                    C.ve(_)
                    need_v = C.vS.n
                    st_done = p3["st_done"][sl]
                    sQs2 = sQ2S[sl]
                    for grp in range(4):
                        bank = psT[grp % 2]
                        pvfree = C.vS.n
                        def _(te, grp=grp, bank=bank, need_v=need_v, qt=qt,
                              pvfree=pvfree):
                            te.wait_ge(C.vS.h, max(need_v, pvfree))
                            for k in range(8):
                                jc = grp * 8 + k
                                i = te.transpose(bank[:, k * 128:(k + 1) * 128],
                                                 qt[:, jc * 128:(jc + 1) * 128],
                                                 ident[:])
                            C.tS.inc(i)
                        C.te(_)
                        need_t = C.tS.n
                        def _(ve, grp=grp, bank=bank, need_t=need_t, stg=stg,
                              st_done=st_done, sQs2=sQs2):
                            ve.wait_ge(C.tS.h, need_t)
                            ve.wait_ge(sQs2.h, st_done)
                            i = ve.tensor_copy(out=stg[:, grp * 1024:(grp + 1) * 1024],
                                               in_=bank[:])
                            C.vS.inc(i)
                        C.ve(_)
                    p3["tern_done"][sl] = C.tS.n
                    need_v2 = C.vS.n
                    def _(gp, dc=dc, half=half, need_v2=need_v2, stg=stg, sQs2=sQs2):
                        gp.wait_ge(C.vS.h, need_v2)
                        gp.dma_start(out=q2d[dc, :, half * 4096:(half + 1) * 4096],
                                     in_=stg[:]).then_inc(sQs2.h, 16)
                    C.gp(_)
                    sQs2.n += 16
                    p3["st_done"][sl] = sQs2.n
                    st.setdefault("q2d_ready", {})[(dc, half)] = (sQs2.h, sQs2.n)

                def emit_mm1_loads(jt):
                    for half in range(2):
                        slot = (jt * 2 + half) % 4
                        buf, sB = q1R[slot], sQ1[slot]
                        last = st["q1_last"][slot]
                        def _(se, buf=buf, jt=jt, half=half, last=last, sB=sB):
                            se.wait_ge(C.tS.h, last)
                            se.dma_start(
                                out=buf[:],
                                in_=q1d[jt, :, half * 4096:(half + 1) * 4096],
                            ).then_inc(sB.h, 16)
                        C.se(_)
                        sB.n += 16

                def emit_mm1_jt(w, jt):
                    ln_v = st["ln_v"][w]
                    needs = [(sQ1[(jt * 2 + h) % 4].h, sQ1[(jt * 2 + h) % 4].n)
                             for h in range(2)]
                    vfree = st.get("psA_free", 0)
                    def _(te, jt=jt, needs=needs, vfree=vfree, ln_v=ln_v):
                        for h_, n_ in needs:
                            te.wait_ge(h_, n_)
                        te.wait_ge(C.vS.h, max(vfree, ln_v))
                        for half in range(2):
                            buf = q1R[(jt * 2 + half) % 4]
                            for s4 in range(NS4):
                                for k in range(8):
                                    dc = half * 8 + k
                                    i = te.matmul(
                                        psA[s4][:],
                                        ynT[:, dc * 512 + s4 * 128:
                                            dc * 512 + (s4 + 1) * 128],
                                        buf[:, k * 512:(k + 1) * 512],
                                        start=(dc == 0), stop=(dc == NDC - 1))
                        C.tS.inc(i)
                    C.te(_)
                    st["q1_last"][(jt * 2) % 4] = C.tS.n
                    st["q1_last"][(jt * 2 + 1) % 4] = C.tS.n
                    need_t = C.tS.n
                    for s4 in range(NS4):
                        ut, sU = utile[s4], sUT[s4]
                        stdone = sU.n
                        def _(ve, s4=s4, jt=jt, w=w, ut=ut, need_t=need_t,
                              sU=sU, stdone=stdone):
                            ve.wait_ge(C.tS.h, need_t)
                            ve.wait_ge(sU.h, stdone)
                            i = ve.tensor_scalar(
                                out=ut[:], in0=psA[s4][:],
                                scalar1=0.0, scalar2=None,
                                op0=OP.max, op1=OP.max,
                                accum_out=mparts[:, (w % 2) * 64 + s4 * NJT + jt:
                                                 (w % 2) * 64 + s4 * NJT + jt + 1])
                            C.vS.inc(i)
                        C.ve(_)
                        need_v = C.vS.n
                        def _(gp, s4=s4, jt=jt, w=w, ut=ut, need_v=need_v, sU=sU):
                            gp.wait_ge(C.vS.h, need_v)
                            gp.dma_start(out=u_d[w * NS4 + s4, :,
                                                 jt * 512:(jt + 1) * 512],
                                         in_=ut[:]).then_inc(sU.h, 16)
                        C.gp(_)
                        sU.n += 16
                    st["psA_free"] = C.vS.n

                def emit_scalars_w(w):
                    vprev = C.vS.n
                    def _(ve, w=w, vprev=vprev):
                        ve.wait_ge(C.vS.h, vprev)
                        for s4 in range(NS4):
                            i = ve.tensor_reduce(
                                out=wvec[:, wv(w, s4):wv(w, s4) + 1],
                                in_=mparts[:, (w % 2) * 64 + s4 * NJT:
                                           (w % 2) * 64 + (s4 + 1) * NJT],
                                axis=AX.X, op=OP.max)
                            ve.drain()
                            i = ve.tensor_mul(out=wvec[:, wv(w, 4 + s4):wv(w, 5 + s4)],
                                              in0=wvec[:, wv(w, s4):wv(w, s4) + 1],
                                              in1=wvec[:, wv(w, s4):wv(w, s4) + 1])
                            ve.drain()
                            i = ve.tensor_max(out=wvec[:, wv(w, 4 + s4):wv(w, 5 + s4)],
                                              in0=wvec[:, wv(w, 4 + s4):wv(w, 5 + s4)],
                                              in1=svec[:, 2:3])
                            ve.drain()
                            i = ve.reciprocal(out=wvec[:, wv(w, 8 + s4):wv(w, 9 + s4)],
                                              in_=wvec[:, wv(w, 4 + s4):wv(w, 5 + s4)])
                            ve.drain()
                            i = ve.tensor_scalar(out=wvec[:, wv(w, 8 + s4):wv(w, 9 + s4)],
                                                 in0=wvec[:, wv(w, 8 + s4):wv(w, 9 + s4)],
                                                 scalar1=255.0, scalar2=None,
                                                 op0=OP.mult)
                            i = ve.tensor_mul(out=wvec[:, wv(w, 16 + s4):wv(w, 17 + s4)],
                                              in0=wvec[:, wv(w, 4 + s4):wv(w, 5 + s4)],
                                              in1=svec[:, 3:4])
                            ve.drain()
                        C.vS.inc(i)
                    C.ve(_)
                    need_v = C.vS.n
                    def _(se, w=w, need_v=need_v):
                        se.wait_ge(C.vS.h, need_v)
                        for s4 in range(NS4):
                            i = se.activation(out=wvec[:, wv(w, 12 + s4):wv(w, 13 + s4)],
                                              in_=wvec[:, wv(w, 8 + s4):wv(w, 9 + s4)],
                                              func=AF.Sqrt, bias=0.0, scale=1.0)
                        C.aS.inc(i)
                    C.se(_)
                    need_a = C.aS.n
                    def _(ve, w=w, need_a=need_a):
                        ve.wait_ge(C.aS.h, need_a)
                        for s4 in range(NS4):
                            i = ve.reciprocal(out=wvec[:, wv(w, 20 + s4):wv(w, 21 + s4)],
                                              in_=wvec[:, wv(w, 12 + s4):wv(w, 13 + s4)])
                            ve.drain()
                            i = ve.tensor_mul(out=wvec[:, wv(w, 20 + s4):wv(w, 21 + s4)],
                                              in0=wvec[:, wv(w, 20 + s4):wv(w, 21 + s4)],
                                              in1=wvec[:, wv(w, 8 + s4):wv(w, 9 + s4)])
                            ve.drain()
                            i = ve.tensor_add(out=wvec[:, wv(w, 20 + s4):wv(w, 21 + s4)],
                                              in0=wvec[:, wv(w, 20 + s4):wv(w, 21 + s4)],
                                              in1=wvec[:, wv(w, 12 + s4):wv(w, 13 + s4)])
                            ve.drain()
                            i = ve.tensor_scalar(out=wvec[:, wv(w, 12 + s4):wv(w, 13 + s4)],
                                                 in0=wvec[:, wv(w, 20 + s4):wv(w, 21 + s4)],
                                                 scalar1=0.5, scalar2=None,
                                                 op0=OP.mult)
                            ve.drain()
                        C.vS.inc(i)
                    C.ve(_)
                    return C.vS.n

                def emit_qrelu2_chunk(w, gchk, sqc_ready):
                    u_done = st["u_done"][w]
                    chk, s4 = divmod(gchk, NS4)
                    p = gchk % 2
                    ub, sU = uR[p], sUR[p]
                    aprev = st["u_sq"][p]
                    def _(se, s4=s4, w=w, chk=chk, ub=ub, sU=sU,
                          u_done=u_done, aprev=aprev):
                        for h_, n_ in u_done:
                            se.wait_ge(h_, n_)
                        se.wait_ge(C.aS.h, aprev)
                        se.dma_start(out=ub[:],
                                     in_=u_d[w * NS4 + s4, :,
                                             chk * QC:(chk + 1) * QC]
                                     ).then_inc(sU.h, 16)
                    C.se(_)
                    sU.n += 16
                    need_d = sU.n
                    vprev = st["dve_p"][p]
                    def _(se, s4=s4, w=w, ub=ub, sU=sU, need_d=need_d,
                          sqc_ready=sqc_ready, vprev=vprev, p=p):
                        se.wait_ge(sU.h, need_d)
                        if p == 0:
                            se.wait_ge(C.vS.h, max(vprev, sqc_ready))
                        else:
                            se.wait_ge(C.gS.h, vprev)
                            se.wait_ge(C.vS.h, sqc_ready)
                        i = se.activation(out=tP[p][:], in_=ub[:],
                                          func=AF.Square, bias=0.0,
                                          scale=wvec[:, wv(w, 12 + s4):
                                                     wv(w, 13 + s4)])
                        C.aS.inc(i)
                    C.se(_)
                    need_a = C.aS.n
                    st["u_sq"][p] = need_a
                    ab, sA = aB[gchk % 4], sAB[gchk % 4]
                    stdone = sA.n
                    if p == 0:
                        def _(ve, need_a=need_a, ab=ab, sA=sA, stdone=stdone,
                              p=p):
                            ve.wait_ge(C.aS.h, need_a)
                            ve.wait_ge(sA.h, stdone)
                            i = ve.tensor_copy(out=k16[p][:], in_=tP[p][:])
                            ve.drain()
                            i = ve.scalar_tensor_tensor(out=tP[p][:],
                                                        in0=k16[p][:],
                                                        scalar=-1.0, in1=tP[p][:],
                                                        op0=OP.mult, op1=OP.add)
                            ve.drain()
                            i = ve.scalar_tensor_tensor(out=ab[:], in0=tP[p][:],
                                                        scalar=0.5, in1=k16[p][:],
                                                        op0=OP.is_ge, op1=OP.add)
                            C.vS.inc(i)
                        C.ve(_)
                        need_v = C.vS.n
                        st["dve_p"][p] = need_v
                        def _(gp, s4=s4, w=w, chk=chk, ab=ab, sA=sA,
                              need_v=need_v):
                            gp.wait_ge(C.vS.h, need_v)
                            t0 = (w * NS4 + s4) * 128
                            gp.dma_start(out=a_d[t0:t0 + 128,
                                                 chk * QC:(chk + 1) * QC],
                                         in_=ab[:]).then_inc(sA.h, 16)
                        C.gp(_)
                        sA.n += 16
                    else:
                        gprev = C.gS.n
                        def _(gp, need_a=need_a, ab=ab, sA=sA, stdone=stdone,
                              p=p, gprev=gprev):
                            gp.wait_ge(C.aS.h, need_a)
                            gp.wait_ge(sA.h, stdone)
                            gp.wait_ge(C.gS.h, gprev)
                            i = gp.tensor_copy(out=k16[p][:], in_=tP[p][:])
                            gp.drain()
                            i = gp.tensor_tensor(out=tP[p][:], in0=tP[p][:],
                                                 in1=k16[p][:], op=OP.subtract)
                            gp.drain()
                            i = gp.tensor_scalar(out=tP[p][:], in0=tP[p][:],
                                                 scalar1=0.5, scalar2=None,
                                                 op0=OP.is_ge)
                            gp.drain()
                            i = gp.tensor_tensor(out=ab[:], in0=tP[p][:],
                                                 in1=k16[p][:], op=OP.add)
                            C.gS.inc(i)
                        C.gp(_)
                        need_g = C.gS.n
                        st["dve_p"][p] = need_g
                        def _(gp, s4=s4, w=w, chk=chk, ab=ab, sA=sA,
                              need_g=need_g):
                            gp.wait_ge(C.gS.h, need_g)
                            t0 = (w * NS4 + s4) * 128
                            gp.dma_start(out=a_d[t0:t0 + 128,
                                                 chk * QC:(chk + 1) * QC],
                                         in_=ab[:]).then_inc(sA.h, 16)
                        C.gp(_)
                        sA.n += 16

                def emit_at_group(w, cgrp, extra=()):
                    a_done = [(sem.h, sem.n) for sem in sAB] + list(extra)
                    mm2_prev = st["mm2_done"]
                    aT = late["aT"]
                    for k in range(8):
                        jc = cgrp * 8 + k
                        def _(sy, jc=jc, k=k, w=w, a_done=a_done, mm2_prev=mm2_prev,
                              aT=aT):
                            if k == 0:
                                for h_, n_ in a_done:
                                    sy.wait_ge(h_, n_)
                                sy.wait_ge(C.tS.h, mm2_prev)
                            sy.dma_start(
                                out=aT[:, jc * 512:(jc + 1) * 512],
                                in_=a_d[w * SW:(w + 1) * SW, jc * 128:(jc + 1) * 128],
                                transpose=True).then_inc(sAT.h, 16)
                        C.sy(_)
                        sAT.n += 16

                def emit_mm2_loads(dblk):
                    for half in range(2):
                        slot = (dblk * 2 + half) % 4
                        buf, sB = late["q2R"][slot], sQ2[slot]
                        last = st["q2_last"][slot]
                        rdy = st.get("q2d_ready", {}).get((dblk, half))
                        def _(se, buf=buf, dblk=dblk, half=half, last=last, sB=sB,
                              rdy=rdy):
                            se.wait_ge(C.tS.h, last)
                            if rdy is not None:
                                se.wait_ge(*rdy)
                            se.dma_start(out=buf[:],
                                         in_=q2d[dblk, :,
                                                 half * 4096:(half + 1) * 4096]
                                         ).then_inc(sB.h, 16)
                        C.se(_)
                        sB.n += 16

                def emit_mm2_dblk(w, dblk, aT_ready):
                    needs = [(sQ2[(dblk * 2 + h) % 4].h, sQ2[(dblk * 2 + h) % 4].n)
                             for h in range(2)]
                    bank = psB[dblk % 2]
                    vfree = st.get("psB_free", 0)
                    aT = late["aT"]
                    q2Rl = late["q2R"]
                    def _(te, dblk=dblk, bank=bank, needs=needs,
                          vfree=vfree, aT_ready=aT_ready, aT=aT, q2Rl=q2Rl):
                        for h_, n_ in needs:
                            te.wait_ge(h_, n_)
                        te.wait_ge(sAT.h, aT_ready)
                        te.wait_ge(C.vS.h, vfree)
                        for jc in range(NJC):
                            buf = q2Rl[(dblk * 2 + jc // 32) % 4]
                            i = te.matmul(bank[:],
                                          buf[:, (jc % 32) * 128:
                                              (jc % 32 + 1) * 128],
                                          aT[:, jc * 512:(jc + 1) * 512],
                                          start=(jc == 0), stop=(jc == NJC - 1))
                        C.tS.inc(i)
                    C.te(_)
                    st["q2_last"][(dblk * 2) % 4] = C.tS.n
                    st["q2_last"][(dblk * 2 + 1) % 4] = C.tS.n
                    need_t = C.tS.n
                    fsl = dblk % 4
                    fslot, sF = fTR[fsl], sFS[fsl]
                    stdone = sF.n
                    def _(ve, fslot=fslot, need_t=need_t, bank=bank,
                          sF=sF, stdone=stdone):
                        ve.wait_ge(C.tS.h, need_t)
                        ve.wait_ge(sF.h, stdone)
                        i = ve.tensor_copy(out=fslot[:], in_=bank[:])
                        C.vS.inc(i)
                    C.ve(_)
                    st["psB_free"] = C.vS.n
                    need_v = C.vS.n
                    def _(gp, dblk=dblk, w=w, fslot=fslot, need_v=need_v, sF=sF):
                        gp.wait_ge(C.vS.h, need_v)
                        gp.dma_start(out=f_d[dblk * 128:(dblk + 1) * 128,
                                             w * SW:(w + 1) * SW],
                                     in_=fslot[:]).then_inc(sF.h, 16)
                    C.gp(_)
                    sF.n += 16

                def emit_residual_s4(w, s4):
                    f_done = st["f_done"][w]
                    t = w * NS4 + s4
                    fslot, sFk = ftok[0], sFTk[0]
                    vfree = C.vS.n
                    def _(sy, t=t, fslot=fslot, f_done=f_done, vfree=vfree,
                          sFk=sFk):
                        for h_, n_ in f_done:
                            sy.wait_ge(h_, n_)
                        sy.wait_ge(C.vS.h, vfree)
                        sy.dma_start(out=fslot[:],
                                     in_=f_d[:, t * 128:(t + 1) * 128],
                                     transpose=True).then_inc(sFk.h, 16)
                    C.sy(_)
                    sFk.n += 16
                    need_fk = sFk.n
                    aprev3 = C.aS.n
                    def _(se, t=t, vfree=vfree, aprev3=aprev3):
                        se.wait_ge(C.vS.h, vfree)
                        se.wait_ge(C.aS.h, aprev3)
                        se.dma_start(out=xr2[:],
                                     in_=src_ap[t * 128:(t + 1) * 128, :]
                                     ).then_inc(sXR.h, 16)
                    C.se(_)
                    sXR.n += 16
                    need_xr = sXR.n
                    xodone = sXO.n
                    def _(ve, w=w, s4=s4, fslot=fslot, sFk=sFk, need_fk=need_fk,
                          need_xr=need_xr, xodone=xodone):
                        ve.wait_ge(sFk.h, need_fk)
                        ve.wait_ge(sXR.h, need_xr)
                        ve.wait_ge(sXO.h, xodone)
                        i = ve.scalar_tensor_tensor(
                            out=xo[:], in0=fslot[:],
                            scalar=wvec[:, wv(w, 16 + s4):wv(w, 17 + s4)],
                            in1=xr2[:], op0=OP.mult, op1=OP.add)
                        C.vS.inc(i)
                    C.ve(_)
                    need_v = C.vS.n
                    def _(gp, t=t, need_v=need_v):
                        gp.wait_ge(C.vS.h, need_v)
                        gp.dma_start(out=dst_ap[t * 128:(t + 1) * 128, :],
                                     in_=xo[:]).then_inc(sXO.h, 16)
                    C.gp(_)
                    sXO.n += 16

                def emit_ln_w(w):
                    for s4 in range(NS4):
                        emit_ln_tile(w * NS4 + s4, src_ap, xr, sXR, ybf, ynT, 512,
                                     ybf[:, 0:QC], psT, s4)
                    st["ln_v"][w] = C.vS.n

                emit_ln_w(0)
                emit_mm1_loads(0)
                for jt in range(NJT):
                    if jt + 1 < NJT:
                        emit_mm1_loads(jt + 1)
                    emit_mm1_jt(0, jt)
                    if with_pass3:
                        emit_pass3_half(2 * jt)
                        emit_pass3_half(2 * jt + 1)
                st["u_done"][0] = [(sem.h, sem.n) for sem in sUT]
                emit_ln_w(1)
                for K in range(NW):
                    sqc_ready = emit_scalars_w(K)
                    if K == 0:
                        if K + 1 < NW:
                            emit_mm1_loads(0)
                            for jt in range(NJT):
                                if jt + 1 < NJT:
                                    emit_mm1_loads(jt + 1)
                                emit_mm1_jt(K + 1, jt)
                                emit_qrelu2_chunk(K, 2 * jt, sqc_ready)
                                emit_qrelu2_chunk(K, 2 * jt + 1, sqc_ready)
                            st["u_done"][K + 1] = [(sem.h, sem.n) for sem in sUT]
                        else:
                            for g in range(NS4 * NQC):
                                emit_qrelu2_chunk(K, g, sqc_ready)
                        p3done = []
                        if with_pass3:
                            p3["es"].close()
                            p3done = [(sem.h, sem.n) for sem in sQ2S]
                        late["aT"] = fec(nc.sbuf_tensor(f"{tag}_aT",
                                                        [128, NJC * 512], BF16))
                        late["q2R"] = [fec(nc.sbuf_tensor(f"{tag}_q2{i}",
                                                          [128, 4096], BF16))
                                       for i in range(4)]
                        for cgrp in range(8):
                            emit_at_group(K, cgrp, extra=p3done)
                    elif K + 1 < NW:
                        emit_mm1_loads(0)
                        for jt in range(NJT):
                            if jt + 1 < NJT:
                                emit_mm1_loads(jt + 1)
                            emit_mm1_jt(K + 1, jt)
                            emit_qrelu2_chunk(K, 2 * jt, sqc_ready)
                            emit_qrelu2_chunk(K, 2 * jt + 1, sqc_ready)
                            if jt % 2 == 1:
                                emit_at_group(K, jt // 2)
                        st["u_done"][K + 1] = [(sem.h, sem.n) for sem in sUT]
                    else:
                        for g in range(NS4 * NQC):
                            emit_qrelu2_chunk(K, g, sqc_ready)
                            if g % 4 == 3:
                                emit_at_group(K, g // 4)
                    aT_ready = sAT.n
                    ln_w = K + 2 if K + 2 < NW else None
                    res_w = K - 1 if K >= 1 else None
                    ln_needa = []
                    emit_mm2_loads(0)
                    for dblk in range(NDC):
                        if dblk + 1 < NDC:
                            emit_mm2_loads(dblk + 1)
                        emit_mm2_dblk(K, dblk, aT_ready)
                        if dblk % 4 == 1 and ln_w is not None:
                            ln_needa.append(ln_tile_stats(
                                ln_w * NS4 + dblk // 4, src_ap, xr, sXR, ybf,
                                ybf[:, 0:QC]))
                        if dblk % 4 == 2 and res_w is not None:
                            emit_residual_s4(res_w, dblk // 4)
                        if dblk % 4 == 3 and ln_w is not None:
                            ln_tile_transpose(ln_needa[dblk // 4], ybf, ynT, 512,
                                              psT, dblk // 4)
                    st["mm2_done"] = C.tS.n
                    st["f_done"][K] = [(sem.h, sem.n) for sem in sFS]
                    if ln_w is not None:
                        st["ln_v"][ln_w] = C.vS.n
                for s4 in range(NS4):
                    emit_residual_s4(NW - 1, s4)
                C.barrier()

        def emit_attn():
            with ExitStack() as esb:
                bec = esb.enter_context
                xr = bec(nc.sbuf_tensor("b_xr", [128, D], F32))
                xo = bec(nc.sbuf_tensor("b_xo", [128, D], F32))
                ybf = bec(nc.sbuf_tensor("b_ybf", [128, D], BF16))
                sXR = C.dsem("sXR")
                sXO = C.dsem("sXO")
                sWS = [C.dsem(f"sWS{i}") for i in range(2)]
                sQT = [C.dsem(f"sQT{i}") for i in range(2)]

                with ExitStack() as esvk:
                    vkec = esvk.enter_context
                    vT = vkec(nc.sbuf_tensor("b_vT", [128, NDC * 2048], BF16))

                    with ExitStack() as esy:
                        yec = esy.enter_context
                        y1T = yec(nc.sbuf_tensor("b_y1T", [128, NDC * 2048], BF16,
                                                 side="right"))
                        wbB = [yec(nc.sbuf_tensor(f"b_wb{i}", [128, NDC * 512], BF16))
                               for i in range(2)]
                        wtmp = [yec(nc.sbuf_tensor(f"b_wt{i}", [128, 512], F32))
                                for i in range(2)]
                        qt_ = [yec(nc.sbuf_tensor(f"b_qt{i}", [128, 512], BF16))
                               for i in range(2)]
                        psT = [yec(nc.psum_tensor(f"b_psT{i}", [128, 1024], BF16))
                               for i in range(2)]
                        psQ = [yec(nc.psum_tensor(f"b_psQ{i}", [128, 512], F32))
                               for i in range(4)]

                        wb_mm_done = [0, 0]
                        cp_done = [0, 0]
                        quarters = ([(wq_h, et, 'q') for et in range(4)]
                                    + [(wk_h, et, 'k') for et in range(4)]
                                    + [(wv_h, et, 'v') for et in range(4)])
                        NQ = len(quarters)
                        slab_v = [0] * NQ

                        def emit_load_step(qi, j):
                            w_h_, et, _k = quarters[qi]
                            slot = qi % 2
                            wb = wbB[slot]
                            tsl = j % 2
                            tmp, sW = wtmp[tsl], sWS[tsl]
                            cfree = cp_done[tsl]
                            def _(sy, tmp=tmp, j=j, et=et, w_h_=w_h_,
                                  cfree=cfree, sW=sW):
                                sy.wait_ge(C.vS.h, cfree)
                                sy.dma_start(
                                    out=tmp[:],
                                    in_=w_h_[j * 128:(j + 1) * 128,
                                             et * 512:(et + 1) * 512]
                                ).then_inc(sW.h, 16)
                            C.sy(_)
                            sW.n += 16
                            need_d = sW.n
                            tfree = wb_mm_done[slot]
                            def _(ve, tmp=tmp, j=j, need_d=need_d,
                                  tfree=tfree, sW=sW, wb=wb):
                                ve.wait_ge(sW.h, need_d)
                                ve.wait_ge(C.tS.h, tfree)
                                i = ve.tensor_copy(
                                    out=wb[:, j * 512:(j + 1) * 512],
                                    in_=tmp[:])
                                C.vS.inc(i)
                            C.ve(_)
                            cp_done[tsl] = C.vS.n
                            if j == NDC - 1:
                                slab_v[qi] = C.vS.n

                        for j in range(NDC):
                            emit_load_step(0, j)

                        ln_done = [0] * NT
                        evac_done = [0, 0, 0, 0]
                        G = 0
                        for qi, (w_h_, et, kind) in enumerate(quarters):
                            slot = qi % 2
                            wb = wbB[slot]
                            for j in range(NT):
                                if qi + 1 < NQ:
                                    emit_load_step(qi + 1, j)
                                if qi == 0:
                                    emit_ln_tile(j, x1d, xr, sXR, ybf, y1T,
                                                 2048, ybf[:, 0:QC], psT, j)
                                    ln_done[j] = C.vS.n
                                bank = psQ[G % 4]
                                needs = max(slab_v[qi], evac_done[G % 4],
                                            ln_done[j if qi == 0 else NT - 1])
                                if kind in ('q', 'k'):
                                    t = j
                                    def _(te, t=t, bank=bank, wb=wb, needs=needs):
                                        te.wait_ge(C.vS.h, needs)
                                        for dc in range(NDC):
                                            i = te.matmul(
                                                bank[:],
                                                y1T[:, dc * 2048 + t * 128:
                                                    dc * 2048 + (t + 1) * 128],
                                                wb[:, dc * 512:(dc + 1) * 512],
                                                start=(dc == 0),
                                                stop=(dc == NDC - 1))
                                        C.tS.inc(i)
                                    C.te(_)
                                else:
                                    fq, sw = divmod(j, 4)
                                    def _(te, fq=fq, sw=sw, bank=bank, wb=wb,
                                          needs=needs):
                                        te.wait_ge(C.vS.h, needs)
                                        for dc in range(NDC):
                                            i = te.matmul(
                                                bank[:],
                                                wb[:, dc * 512 + fq * 128:
                                                   dc * 512 + (fq + 1) * 128],
                                                y1T[:, dc * 2048 + sw * 512:
                                                    dc * 2048 + (sw + 1) * 512],
                                                start=(dc == 0),
                                                stop=(dc == NDC - 1))
                                        C.tS.inc(i)
                                    C.te(_)
                                wb_mm_done[slot] = C.tS.n
                                need_t = C.tS.n
                                if kind == 'v':
                                    ft = et * 4 + fq
                                    def _(ve, ft=ft, sw=sw, bank=bank,
                                          need_t=need_t):
                                        ve.wait_ge(C.tS.h, need_t)
                                        i = ve.tensor_copy(
                                            out=vT[:, ft * 2048 + sw * 512:
                                                   ft * 2048 + (sw + 1) * 512],
                                            in_=bank[:])
                                        C.vS.inc(i)
                                    C.ve(_)
                                else:
                                    t = j
                                    qsl = G % 2
                                    qb, sQs = qt_[qsl], sQT[qsl]
                                    stdone = sQs.n
                                    def _(ve, kind=kind, bank=bank, qb=qb,
                                          need_t=need_t, sQs=sQs, stdone=stdone):
                                        ve.wait_ge(C.tS.h, need_t)
                                        ve.wait_ge(sQs.h, stdone)
                                        if kind == 'q':
                                            i = ve.tensor_scalar(
                                                out=qb[:], in0=bank[:],
                                                scalar1=0.125, scalar2=None,
                                                op0=OP.mult)
                                        else:
                                            i = ve.tensor_copy(out=qb[:],
                                                               in_=bank[:])
                                        C.vS.inc(i)
                                    C.ve(_)
                                    need_v = C.vS.n
                                    def _(gp, kind=kind, t=t, et=et, qb=qb,
                                          need_v=need_v, sQs=sQs):
                                        gp.wait_ge(C.vS.h, need_v)
                                        if kind == 'q':
                                            gp.dma_start(
                                                out=qd[t * 128:(t + 1) * 128,
                                                       et * 512:(et + 1) * 512],
                                                in_=qb[:]
                                            ).then_inc(sQs.h, 16)
                                        else:
                                            gp.dma_start(
                                                out=kd[:, t * 2048 + et * 512:
                                                       t * 2048 + (et + 1) * 512],
                                                in_=qb[:]
                                            ).then_inc(sQs.h, 16)
                                    C.gp(_)
                                    sQs.n += 16
                                evac_done[G % 4] = C.vS.n
                                G += 1
                        qk_done = [(sem.h, sem.n) for sem in sQT]
                        C.barrier()

                    with ExitStack() as eso:
                        oec = eso.enter_context
                        kT = oec(nc.sbuf_tensor("b_kT", [128, NT * 2048], BF16,
                                                side="right"))
                        qcb = [oec(nc.sbuf_tensor(f"b_qc{i}", [128, NT * 128], BF16))
                               for i in range(2)]
                        wsb = oec(nc.sbuf_tensor("b_wsb", [128, 2048], BF16))
                        wTc = [oec(nc.sbuf_tensor(f"b_wTc{i}", [128, 2048], BF16))
                               for i in range(2)]
                        oR = [oec(nc.sbuf_tensor(f"b_oR{i}", [128, 512], BF16))
                              for i in range(2)]
                        psS = oec(nc.psum_tensor("b_psS", [128, 2048], F32))
                        psO = [oec(nc.psum_tensor(f"b_psO{i}", [128, 512], F32))
                               for i in range(2)]
                        psW = oec(nc.psum_tensor("b_psW", [128, 1024], BF16))
                        sKT = C.dsem("sKT")
                        sQCb = [C.dsem(f"sQC{i}") for i in range(2)]
                        sOR = [C.dsem(f"sOR{i}") for i in range(2)]

                        def _(sy, qk_done=qk_done):
                            for h_, n_ in qk_done:
                                sy.wait_ge(h_, n_)
                            sy.dma_start(out=kT[:], in_=kd).then_inc(sKT.h, 16)
                        C.sy(_)
                        sKT.n += 16
                        kT_ld = sKT.n

                        qsc_last = [0, 0]
                        wT_prev = None
                        def emit_omm(e, wT_v, rs_ready, wTcb):
                            for sw in range(4):
                                bank = psO[sw % 2]
                                vfree = C.vS.n
                                def _(te, bank=bank, sw=sw, wT_v=wT_v, vfree=vfree,
                                      wTcb=wTcb):
                                    te.wait_ge(C.vS.h, max(wT_v, vfree))
                                    for fc in range(NT):
                                        i = te.matmul(
                                            bank[:],
                                            wTcb[:, fc * 128:(fc + 1) * 128],
                                            vT[:, fc * 2048 + sw * 512:
                                               fc * 2048 + (sw + 1) * 512],
                                            start=(fc == 0), stop=(fc == NT - 1))
                                    C.tS.inc(i)
                                C.te(_)
                                need_t = C.tS.n
                                osl = sw % 2
                                ob, sO = oR[osl], sOR[osl]
                                stdone = sO.n
                                def _(ve, e=e, ob=ob, bank=bank, need_t=need_t,
                                      rs_ready=rs_ready, sO=sO, stdone=stdone):
                                    ve.wait_ge(C.tS.h, need_t)
                                    ve.wait_ge(C.vS.h, rs_ready)
                                    ve.wait_ge(sO.h, stdone)
                                    i = ve.tensor_scalar(
                                        out=ob[:], in0=bank[:],
                                        scalar1=wvec[:, 24 + (e % 2) * 32 + 2:
                                                     24 + (e % 2) * 32 + 3],
                                        scalar2=None, op0=OP.mult)
                                    C.vS.inc(i)
                                C.ve(_)
                                need_v = C.vS.n
                                def _(gp, e=e, sw=sw, ob=ob, need_v=need_v, sO=sO):
                                    gp.wait_ge(C.vS.h, need_v)
                                    gp.dma_start(
                                        out=od[:, e * 2048 + sw * 512:
                                               e * 2048 + (sw + 1) * 512],
                                        in_=ob[:]).then_inc(sO.h, 16)
                                C.gp(_)
                                sO.n += 16

                        for e in range(NT):
                            wb = 24 + (e % 2) * 32
                            qc = qcb[e % 2]
                            sQC = sQCb[e % 2]
                            tfree = qsc_last[e % 2]
                            def _(sy, e=e, qc=qc, tfree=tfree, sQC=sQC):
                                sy.wait_ge(C.tS.h, tfree)
                                src = qd[:, e * 128:(e + 1) * 128].rearrange(
                                    "(sc p) c -> p sc c", p=128)
                                dst = qc[:].rearrange("p (sc c) -> p sc c", c=128)
                                sy.dma_start(out=dst, in_=src).then_inc(sQC.h, 16)
                            C.sy(_)
                            sQC.n += 16
                            need_d = sQC.n
                            aprev = C.aS.n
                            def _(te, qc=qc, sQC=sQC, need_d=need_d, aprev=aprev,
                                  kT_ld=kT_ld):
                                te.wait_ge(sQC.h, need_d)
                                te.wait_ge(sKT.h, kT_ld)
                                te.wait_ge(C.aS.h, aprev)
                                for sc in range(NT):
                                    for ft in range(4):
                                        i = te.matmul(
                                            psS[:, ft * 512:(ft + 1) * 512],
                                            qc[:, sc * 128:(sc + 1) * 128],
                                            kT[:, sc * 2048 + ft * 512:
                                               sc * 2048 + (ft + 1) * 512],
                                            start=(sc == 0), stop=(sc == NT - 1))
                                C.tS.inc(i)
                            C.te(_)
                            qsc_last[e % 2] = C.tS.n
                            need_t = C.tS.n
                            def _(ve, wb=wb, need_t=need_t):
                                ve.wait_ge(C.tS.h, need_t)
                                i = ve.tensor_reduce(out=wvec[:, wb:wb + 1], in_=psS[:],
                                                     axis=AX.X, op=OP.max, negate=True)
                                C.vS.inc(i)
                            C.ve(_)
                            need_v = C.vS.n
                            tprev = C.tS.n
                            def _(se, wb=wb, need_v=need_v, tprev=tprev):
                                se.wait_ge(C.vS.h, need_v)
                                se.wait_ge(C.tS.h, tprev)
                                i = se.activation(out=wsb[:], in_=psS[:], func=AF.Exp,
                                                  bias=wvec[:, wb:wb + 1], scale=1.0,
                                                  accum_out=wvec[:, wb + 1:wb + 2])
                                C.aS.inc(i)
                            C.se(_)
                            need_a = C.aS.n
                            def _(ve, wb=wb, need_a=need_a):
                                ve.wait_ge(C.aS.h, need_a)
                                i = ve.reciprocal(out=wvec[:, wb + 2:wb + 3],
                                                  in_=wvec[:, wb + 1:wb + 2])
                                C.vS.inc(i)
                            C.ve(_)
                            rs_ready = C.vS.n
                            wTcb = wTc[e % 2]
                            for grp in range(2):
                                vfree = C.vS.n
                                def _(te, grp=grp, need_a=need_a, vfree=vfree):
                                    te.wait_ge(C.aS.h, need_a)
                                    te.wait_ge(C.vS.h, vfree)
                                    for k in range(8):
                                        fc = grp * 8 + k
                                        i = te.transpose(
                                            psW[:, k * 128:(k + 1) * 128],
                                            wsb[:, fc * 128:(fc + 1) * 128],
                                            ident[:])
                                    C.tS.inc(i)
                                C.te(_)
                                need_t = C.tS.n
                                def _(ve, grp=grp, need_t=need_t, wTcb=wTcb):
                                    ve.wait_ge(C.tS.h, need_t)
                                    i = ve.tensor_copy(
                                        out=wTcb[:, grp * 1024:(grp + 1) * 1024],
                                        in_=psW[:])
                                    C.vS.inc(i)
                                C.ve(_)
                            if wT_prev is not None:
                                emit_omm(*wT_prev)
                            wT_prev = (e, C.vS.n, rs_ready, wTcb)
                        emit_omm(*wT_prev)
                        C.barrier()

                with ExitStack() as es3:
                    tec = es3.enter_context
                    o2 = tec(nc.sbuf_tensor("b_o2", [128, NDC * 2048], BF16,
                                            side="right"))
                    wob = tec(nc.sbuf_tensor("b_wob", [128, NDC * 2048], BF16))
                    wtmp = [tec(nc.sbuf_tensor(f"b_wtmp{i}", [128, 2048], F32))
                            for i in range(2)]
                    psP = [tec(nc.psum_tensor(f"b_psP{i}", [128, 512], F32))
                           for i in range(2)]
                    sO2 = C.dsem("sO2")
                    sWT = [C.dsem(f"sWT{i}") for i in range(2)]

                    def _(sy):
                        sy.dma_start(out=o2[:], in_=od).then_inc(sO2.h, 16)
                    C.sy(_)
                    sO2.n += 16
                    o2_ld = sO2.n
                    for ecx in range(NDC):
                        sl = ecx % 2
                        slot, sW = wtmp[sl], sWT[sl]
                        vfree = C.vS.n
                        def _(sy, slot=slot, ecx=ecx, vfree=vfree, sW=sW):
                            sy.wait_ge(C.vS.h, max(0, vfree - 1))
                            sy.dma_start(out=slot[:],
                                         in_=wo_h[ecx * 128:(ecx + 1) * 128, :]
                                         ).then_inc(sW.h, 16)
                        C.sy(_)
                        sW.n += 16
                        need_d = sW.n
                        def _(ve, slot=slot, ecx=ecx, need_d=need_d, sW=sW):
                            ve.wait_ge(sW.h, need_d)
                            i = ve.tensor_copy(out=wob[:, ecx * 2048:(ecx + 1) * 2048],
                                               in_=slot[:])
                            C.vS.inc(i)
                        C.ve(_)
                    wo_v = C.vS.n

                    for t in range(NT):
                        vfree = C.vS.n
                        aprev = C.aS.n
                        def _(se, t=t, vfree=vfree, aprev=aprev):
                            se.wait_ge(C.vS.h, vfree)
                            se.wait_ge(C.aS.h, aprev)
                            se.dma_start(out=xr[:], in_=x1d[t * 128:(t + 1) * 128, :]
                                         ).then_inc(sXR.h, 16)
                        C.se(_)
                        sXR.n += 16
                        need_xr = sXR.n
                        for gt in range(4):
                            bank = psP[gt % 2]
                            vfree = C.vS.n
                            def _(te, t=t, gt=gt, bank=bank, wo_v=wo_v,
                                  vfree=vfree, o2_ld=o2_ld, sO2=sO2):
                                te.wait_ge(C.vS.h, max(wo_v, vfree))
                                te.wait_ge(sO2.h, o2_ld)
                                for ecx in range(NDC):
                                    i = te.matmul(
                                        bank[:],
                                        o2[:, ecx * 2048 + t * 128:
                                           ecx * 2048 + (t + 1) * 128],
                                        wob[:, ecx * 2048 + gt * 512:
                                            ecx * 2048 + (gt + 1) * 512],
                                        start=(ecx == 0), stop=(ecx == NDC - 1))
                                C.tS.inc(i)
                            C.te(_)
                            need_t = C.tS.n
                            xodone = sXO.n
                            def _(ve, gt=gt, bank=bank, need_t=need_t,
                                  need_xr=need_xr, xodone=xodone):
                                ve.wait_ge(C.tS.h, need_t)
                                ve.wait_ge(sXR.h, need_xr)
                                ve.wait_ge(sXO.h, xodone)
                                i = ve.tensor_add(out=xo[:, gt * 512:(gt + 1) * 512],
                                                  in0=bank[:],
                                                  in1=xr[:, gt * 512:(gt + 1) * 512])
                                C.vS.inc(i)
                            C.ve(_)
                        need_v = C.vS.n
                        def _(gp, t=t, need_v=need_v):
                            gp.wait_ge(C.vS.h, need_v)
                            gp.dma_start(out=x2d[t * 128:(t + 1) * 128, :],
                                         in_=xo[:]).then_inc(sXO.h, 16)
                        C.gp(_)
                        sXO.n += 16
                    C.barrier()

        emit_prep()
        emit_ffn(x_h, x1d, "A", with_pass3=True)
        emit_attn()
        emit_ffn(x2d, out_h, "Cc")

        snap = [(s.h, s.n) for s in C.all_sems if s.n]
        def _(sy, snap=snap):
            for h, n in snap:
                sy.wait_ge(h, n)
        C.sy(_)

    return nc


def kernel(L0_attn_x, L0_attn_Wq, L0_attn_Wk, L0_attn_Wv, L0_attn_Wo,
           L0_ffn_c_fc, L0_ffn_c_proj):
    bf16 = ml_dtypes.bfloat16
    x = np.ascontiguousarray(np.asarray(L0_attn_x, dtype=np.float32))
    ident = np.eye(128, dtype=bf16)
    ones = np.ones((128, 128), dtype=np.float32)
    common = {
        "Wq": np.ascontiguousarray(np.asarray(L0_attn_Wq, dtype=np.float32)),
        "Wk": np.ascontiguousarray(np.asarray(L0_attn_Wk, dtype=np.float32)),
        "Wv": np.ascontiguousarray(np.asarray(L0_attn_Wv, dtype=np.float32)),
        "Wo": np.ascontiguousarray(np.asarray(L0_attn_Wo, dtype=np.float32)),
        "cfc": np.ascontiguousarray(np.asarray(L0_ffn_c_fc, dtype=np.float32)),
        "cpj": np.ascontiguousarray(np.asarray(L0_ffn_c_proj, dtype=np.float32)),
        "identin": ident, "onesin": ones,
    }
    in_maps = [{"x": x[b], **common} for b in range(8)]
    nc = build_program()
    res = run_bass_kernel_spmd(nc, in_maps, core_ids=list(range(8)))
    return np.stack([np.asarray(res.results[b]["out"], dtype=np.float32)
                     for b in range(8)], axis=0)


# version 24 (final)
# speedup vs baseline: 1.1913x; 1.1913x over previous
import numpy as np
import ml_dtypes
from contextlib import ExitStack

import concourse.bass as bass
import concourse.mybir as mybir
from concourse.bass_utils import run_bass_kernel_spmd
from concourse.alu_op_type import AluOpType as OP

F32 = mybir.dt.float32
BF16 = mybir.dt.bfloat16
FP16 = mybir.dt.float16
I32 = mybir.dt.int32
AF = mybir.ActivationFunctionType
AX = mybir.AxisListType

S = 2048
D = 2048
H = 8192
NT = S // 128
NDC = D // 128
SW = 512
NW = S // SW
NS4 = SW // 128
NJT = H // 512
NJC = H // 128
QC = 1024
NQC = H // QC

LN_EPS = 1e-5
T_EPS = 1e-5
NELEM_W = float(D * H)


class Sem:
    def __init__(self, es, nc, name):
        self.h = es.enter_context(nc.semaphore(name))
        self.n = 0

    def inc(self, instr, k=1):
        instr.then_inc(self.h, k)
        self.n += k
        return self.n


class Ctx:
    def __init__(self, nc, es):
        self.nc = nc
        self.es = es
        self.block = es.enter_context(nc.Block())
        self.all_sems = []
        self.tS = self.sem("tS")
        self.vS = self.sem("vS")
        self.aS = self.sem("aS")
        self._dsems = {}

    def sem(self, name):
        s = Sem(self.es, self.nc, name)
        self.all_sems.append(s)
        return s

    def dsem(self, name):
        if name not in self._dsems:
            self._dsems[name] = self.sem(name)
        return self._dsems[name]

    def te(self, f): self.block.tensor(f)
    def ve(self, f): self.block.vector(f)
    def se(self, f): self.block.scalar(f)
    def sy(self, f): self.block.sync(f)
    def gp(self, f): self.block.gpsimd(f)

    def barrier(self):
        snap = [(s.h, s.n) for s in self.all_sems if s.n]
        for sec in (self.te, self.ve, self.se, self.sy, self.gp):
            def _(eng, snap=snap):
                for h, n in snap:
                    eng.wait_ge(h, n)
            sec(_)


def build_program():
    nc = bass.Bass("TRN2", target_bir_lowering=False, debug=False)

    x_h = nc.dram_tensor("x", [S, D], F32, kind="ExternalInput").ap()
    wq_h = nc.dram_tensor("Wq", [D, D], F32, kind="ExternalInput").ap()
    wk_h = nc.dram_tensor("Wk", [D, D], F32, kind="ExternalInput").ap()
    wv_h = nc.dram_tensor("Wv", [D, D], F32, kind="ExternalInput").ap()
    wo_h = nc.dram_tensor("Wo", [D, D], F32, kind="ExternalInput").ap()
    cfc_h = nc.dram_tensor("cfc", [D, H], F32, kind="ExternalInput").ap()
    cpj_h = nc.dram_tensor("cpj", [D, H], F32, kind="ExternalInput").ap()
    id_h = nc.dram_tensor("identin", [128, 128], BF16, kind="ExternalInput").ap()
    ones_h = nc.dram_tensor("onesin", [128, 128], F32, kind="ExternalInput").ap()
    out_h = nc.dram_tensor("out", [S, D], F32, kind="ExternalOutput").ap()

    q1d = nc.dram_tensor("q1d", [NJT, 128, NDC * 512], BF16).ap()
    q2d = nc.dram_tensor("q2d", [NDC, 128, NJC * 128], BF16).ap()
    x1d = nc.dram_tensor("x1d", [S, D], F32).ap()
    x2d = nc.dram_tensor("x2d", [S, D], F32).ap()
    qd = nc.dram_tensor("qd", [S, D], BF16).ap()
    kd = nc.dram_tensor("kd", [128, NT * 2048], BF16).ap()
    od = nc.dram_tensor("od", [128, NDC * 2048], BF16).ap()
    u_d = nc.dram_tensor("u_d", [NT, 128, H], FP16).ap()
    a_d = nc.dram_tensor("a_d", [S, H], BF16).ap()
    f_d = nc.dram_tensor("f_d", [D, S], BF16).ap()

    with ExitStack() as es0:
        C = Ctx(nc, es0)
        ec = es0.enter_context

        ident = ec(nc.sbuf_tensor("g_ident", [128, 128], BF16))
        ones = ec(nc.sbuf_tensor("g_ones", [128, 128], F32))
        svec = ec(nc.sbuf_tensor("g_svec", [128, 16], F32))
        mparts = ec(nc.sbuf_tensor("g_mparts", [128, 128], F32))
        wvec = ec(nc.sbuf_tensor("g_wvec", [128, 64], F32))
        lnv = ec(nc.sbuf_tensor("g_lnv", [128, 16], F32))
        s11 = ec(nc.sbuf_tensor("g_s11", [1, 16], F32))

        sGlob = C.dsem("sGlob")

        def _(sy):
            sy.dma_start(out=ident[:], in_=id_h).then_inc(sGlob.h, 16)
            sy.dma_start(out=ones[:], in_=ones_h).then_inc(sGlob.h, 16)
        C.sy(_)
        sGlob.n += 32

        def emit_prep():
            with ExitStack() as es:
                pec = es.enter_context
                wbuf = [pec(nc.sbuf_tensor(f"p_w{i}", [128, H], F32)) for i in range(4)]
                parts = pec(nc.sbuf_tensor("p_parts", [128, 2], F32))
                pmat = pec(nc.sbuf_tensor("p_pmat", [128, 32], F32))
                junk = pec(nc.sbuf_tensor("p_junk", [128, H], BF16))
                qtB = [pec(nc.sbuf_tensor(f"p_qt{i}", [128, H], BF16)) for i in range(2)]
                ntmp = pec(nc.sbuf_tensor("p_ntmp", [128, H], BF16))
                psV = pec(nc.psum_tensor("p_psV", [128, 128], F32))
                sQ1S = [C.dsem(f"sQ1S{i}") for i in range(2)]

                QMAP = {'sy': C.sy, 'se': C.se, 'gp': C.gp}
                sem_pool = {}

                def q_sem(qname, sl):
                    key = (qname, sl)
                    if key not in sem_pool:
                        sem_pool[key] = C.dsem(f"sP_{qname}{sl}")
                    return sem_pool[key]

                buf_free = {sl: None for sl in range(4)}

                def load_chunks(src, dc, sl, chunks, extra_wait=None):
                    buf = wbuf[sl]
                    wfree = buf_free[sl]
                    needs = []
                    for qn, c0, c1 in chunks:
                        sB = q_sem(qn, sl)
                        def _(eng, buf=buf, src=src, dc=dc, c0=c0, c1=c1,
                              wfree=wfree, sB=sB, extra_wait=extra_wait):
                            if wfree:
                                eng.wait_ge(*wfree)
                            if extra_wait:
                                eng.wait_ge(*extra_wait)
                            eng.dma_start(
                                out=buf[:, c0:c1],
                                in_=src[dc * 128:(dc + 1) * 128, c0:c1]
                            ).then_inc(sB.h, 16)
                        QMAP[qn](_)
                        sB.n += 16
                        needs.append((sB.h, sB.n))
                    return buf, needs

                def reduce_tile(buf, needs, wi, dc):
                    aprev = C.aS.n
                    def _(se, buf=buf, needs=tuple(needs), wi=wi, dc=dc,
                          aprev=aprev):
                        for h, v in needs:
                            se.wait_ge(h, v)
                        se.wait_ge(C.aS.h, aprev)
                        i = se.activation(out=junk[:], in_=buf[:], func=AF.Abs,
                                          bias=0.0, scale=1.0,
                                          accum_out=pmat[:, wi * 16 + dc:
                                                         wi * 16 + dc + 1])
                        C.aS.inc(i)
                    C.se(_)

                def reduce_finish(wi):
                    aneed = C.aS.n
                    def _(ve, wi=wi, aneed=aneed):
                        ve.wait_ge(C.aS.h, aneed)
                        i = ve.tensor_reduce(out=parts[:, wi:wi + 1],
                                             in_=pmat[:, wi * 16:(wi + 1) * 16],
                                             axis=AX.X, op=OP.add)
                        ve.drain()
                        C.vS.inc(i)
                    C.ve(_)

                def scal_part(wi):
                    vneed = C.vS.n
                    def _(te, vneed=vneed, wi=wi):
                        te.wait_ge(C.vS.h, vneed)
                        i = te.matmul(psV[0:1, wi:wi + 1], parts[:, wi:wi + 1],
                                      ones[:, 0:1], start=True, stop=True)
                        C.tS.inc(i)
                    C.te(_)
                    tneed = C.tS.n
                    def _(ve, tneed=tneed, wi=wi):
                        ve.wait_ge(C.tS.h, tneed)
                        i = ve.tensor_scalar(out=s11[0:1, wi:wi + 1],
                                             in0=psV[0:1, wi:wi + 1],
                                             scalar1=1.0 / NELEM_W, scalar2=T_EPS,
                                             op0=OP.mult, op1=OP.max)
                        C.vS.inc(i)
                    C.ve(_)
                    vneed = C.vS.n
                    def _(te, vneed=vneed, wi=wi):
                        te.wait_ge(C.vS.h, vneed)
                        i = te.matmul(psV[:, 2 + wi:3 + wi], ones[0:1, :],
                                      s11[0:1, wi:wi + 1], start=True, stop=True)
                        C.tS.inc(i)
                    C.te(_)
                    tneed = C.tS.n
                    def _(ve, tneed=tneed, wi=wi):
                        ve.wait_ge(C.tS.h, tneed)
                        i = ve.tensor_copy(out=svec[:, 6 + wi:7 + wi],
                                           in_=psV[:, 2 + wi:3 + wi])
                        ve.drain()
                        sb = svec[:, 6 + wi:7 + wi]
                        if wi == 0:
                            i = ve.tensor_scalar(out=svec[:, 0:1], in0=sb,
                                                 scalar1=0.5, scalar2=None, op0=OP.mult)
                            i = ve.tensor_scalar(out=svec[:, 1:2], in0=sb,
                                                 scalar1=-0.5, scalar2=None, op0=OP.mult)
                            i = ve.tensor_mul(out=wvec[:, 30:31], in0=sb, in1=sb)
                            ve.drain()
                            i = ve.reciprocal(out=wvec[:, 31:32], in_=wvec[:, 30:31])
                            ve.drain()
                            i = ve.tensor_scalar(out=svec[:, 2:3], in0=wvec[:, 31:32],
                                                 scalar1=T_EPS, scalar2=None,
                                                 op0=OP.mult)
                        else:
                            i = ve.tensor_scalar(out=svec[:, 4:5], in0=sb,
                                                 scalar1=0.5, scalar2=None, op0=OP.mult)
                            i = ve.tensor_scalar(out=svec[:, 5:6], in0=sb,
                                                 scalar1=-0.5, scalar2=None, op0=OP.mult)
                            i = ve.tensor_mul(out=wvec[:, 29:30], in0=wvec[:, 30:31],
                                              in1=sb)
                            ve.drain()
                            i = ve.tensor_scalar(out=svec[:, 3:4], in0=wvec[:, 29:30],
                                                 scalar1=0.5 / 255.0, scalar2=None,
                                                 op0=OP.mult)
                        C.vS.inc(i)
                    C.ve(_)
                    return C.vS.n

                def tern_store_q1(buf, needs, scal_ready, dc):
                    qt = qtB[dc % 2]
                    sQs = sQ1S[dc % 2]
                    st_done = sQs.n
                    vprev = C.vS.n
                    def _(ve, buf=buf, needs=tuple(needs), st_done=st_done,
                          scal_ready=scal_ready, qt=qt, sQs=sQs, vprev=vprev):
                        for h, v in needs:
                            ve.wait_ge(h, v)
                        ve.wait_ge(C.vS.h, max(scal_ready, vprev))
                        ve.wait_ge(sQs.h, st_done)
                        i = ve.tensor_scalar(out=ntmp[:], in0=buf[:],
                                             scalar1=svec[:, 1:2], scalar2=None,
                                             op0=OP.is_lt)
                        ve.drain()
                        i = ve.scalar_tensor_tensor(out=qt[:], in0=buf[:],
                                                    scalar=svec[:, 0:1], in1=ntmp[:],
                                                    op0=OP.is_gt, op1=OP.subtract)
                        C.vS.inc(i)
                    C.ve(_)
                    return C.vS.n, qt

                CH3 = (('sy', 0, 2730), ('se', 2730, 5460), ('gp', 5460, H))
                PF = 3

                def emit_1a_load(dc):
                    return load_chunks(cfc_h, dc, dc % 4, CH3,
                                       (sGlob.h, sGlob.n) if dc < PF else None)
                pend = {dc: emit_1a_load(dc) for dc in range(PF)}
                for dc in range(NDC):
                    if dc + PF < NDC:
                        pend[dc + PF] = emit_1a_load(dc + PF)
                    buf, needs = pend.pop(dc)
                    reduce_tile(buf, needs, 0, dc)
                    buf_free[dc % 4] = (C.aS.h, C.aS.n)
                reduce_finish(0)
                s1_ready = scal_part(0)

                gp_cfc = {1, 4, 7, 10, 13}
                sy_cpj = {6, 12}
                for dc in range(NDC):
                    sl = dc % 2
                    qn = 'gp' if dc in gp_cfc else 'sy'
                    buf, needs = load_chunks(cfc_h, dc, sl, ((qn, 0, H),))
                    need_v, qt = tern_store_q1(buf, needs, s1_ready, dc)
                    buf_free[sl] = (C.vS.h, need_v)
                    sQs = sQ1S[dc % 2]
                    def _(gp, dc=dc, need_v=need_v, qt=qt, sQs=sQs):
                        gp.wait_ge(C.vS.h, need_v)
                        dst = q1d[:, :, dc * 512:(dc + 1) * 512].rearrange(
                            "jt p c -> p jt c")
                        src = qt[:].rearrange("p (jt c) -> p jt c", c=512)
                        gp.dma_start(out=dst, in_=src).then_inc(sQs.h, 16)
                    C.gp(_)
                    sQs.n += 16
                    qn2 = 'sy' if dc in sy_cpj else 'se'
                    buf2, needs2 = load_chunks(cpj_h, dc, 2 + sl, ((qn2, 0, H),))
                    reduce_tile(buf2, needs2, 1, dc)
                    buf_free[2 + sl] = (C.aS.h, C.aS.n)
                reduce_finish(1)
                s2_ready = scal_part(1)

                C.barrier()

        def ln_tile_stats(t, src_ap, xr, sXR, ybf, sq_scr):
            vfree = C.vS.n
            aprev = C.aS.n
            def _(se, t=t, vfree=vfree, aprev=aprev):
                se.wait_ge(C.vS.h, vfree)
                se.wait_ge(C.aS.h, aprev)
                se.dma_start(out=xr[:], in_=src_ap[t * 128:(t + 1) * 128, :]
                             ).then_inc(sXR.h, 16)
            C.se(_)
            sXR.n += 16
            need_d = sXR.n
            vprev = C.vS.n
            def _(se, need_d=need_d, vprev=vprev):
                se.wait_ge(sXR.h, need_d)
                se.wait_ge(C.vS.h, vprev)
                i = se.activation(out=sq_scr, in_=xr[:, 0:QC], func=AF.Square,
                                  bias=0.0, scale=1.0, accum_out=lnv[:, 0:1])
                se.drain()
                i = se.activation(out=sq_scr, in_=xr[:, QC:2 * QC], func=AF.Square,
                                  bias=0.0, scale=1.0, accum_out=lnv[:, 1:2])
                C.aS.inc(i)
            C.se(_)
            need_a = C.aS.n
            def _(ve, need_d=need_d, need_a=need_a):
                ve.wait_ge(sXR.h, need_d)
                ve.wait_ge(C.aS.h, need_a)
                i = ve.tensor_reduce(out=lnv[:, 2:3], in_=xr[:], axis=AX.X, op=OP.add)
                i = ve.tensor_add(out=lnv[:, 4:5], in0=lnv[:, 0:1], in1=lnv[:, 1:2])
                ve.drain()
                i = ve.tensor_scalar(out=lnv[:, 3:4], in0=lnv[:, 2:3],
                                     scalar1=1.0 / D, scalar2=None, op0=OP.mult)
                i = ve.tensor_scalar(out=lnv[:, 4:5], in0=lnv[:, 4:5],
                                     scalar1=1.0 / D, scalar2=None, op0=OP.mult)
                ve.drain()
                i = ve.tensor_mul(out=lnv[:, 5:6], in0=lnv[:, 3:4], in1=lnv[:, 3:4])
                ve.drain()
                i = ve.tensor_sub(out=lnv[:, 6:7], in0=lnv[:, 4:5], in1=lnv[:, 5:6])
                ve.drain()
                i = ve.tensor_scalar(out=lnv[:, 6:7], in0=lnv[:, 6:7],
                                     scalar1=LN_EPS, scalar2=None, op0=OP.add)
                C.vS.inc(i)
            C.ve(_)
            need_v = C.vS.n
            def _(se, need_v=need_v):
                se.wait_ge(C.vS.h, need_v)
                i = se.activation(out=lnv[:, 7:8], in_=lnv[:, 6:7], func=AF.Sqrt,
                                  bias=0.0, scale=1.0)
                C.aS.inc(i)
            C.se(_)
            need_a = C.aS.n
            def _(ve, need_a=need_a):
                ve.wait_ge(C.aS.h, need_a)
                i = ve.reciprocal(out=lnv[:, 8:9], in_=lnv[:, 7:8])
                ve.drain()
                i = ve.tensor_mul(out=lnv[:, 9:10], in0=lnv[:, 8:9], in1=lnv[:, 8:9])
                ve.drain()
                i = ve.tensor_mul(out=lnv[:, 9:10], in0=lnv[:, 9:10], in1=lnv[:, 6:7])
                ve.drain()
                i = ve.tensor_scalar(out=lnv[:, 9:10], in0=lnv[:, 9:10],
                                     scalar1=-0.5, scalar2=1.5, op0=OP.mult, op1=OP.add)
                ve.drain()
                i = ve.tensor_mul(out=lnv[:, 10:11], in0=lnv[:, 8:9], in1=lnv[:, 9:10])
                ve.drain()
                i = ve.tensor_mul(out=lnv[:, 11:12], in0=lnv[:, 3:4], in1=lnv[:, 10:11])
                ve.drain()
                i = ve.tensor_scalar(out=lnv[:, 11:12], in0=lnv[:, 11:12],
                                     scalar1=-1.0, scalar2=None, op0=OP.mult)
                C.vS.inc(i)
            C.ve(_)
            need_v = C.vS.n
            tprev = C.tS.n
            def _(se, need_v=need_v, tprev=tprev):
                se.wait_ge(C.vS.h, need_v)
                se.wait_ge(C.tS.h, tprev)
                i = se.activation(out=ybf[:], in_=xr[:], func=AF.Identity,
                                  bias=lnv[:, 11:12], scale=lnv[:, 10:11])
                C.aS.inc(i)
            C.se(_)
            return C.aS.n

        def ln_tile_transpose(need_a, ybf, yt, ytile, psT, slot):
            for grp in range(2):
                bank = psT[grp]
                vfree = C.vS.n
                def _(te, grp=grp, bank=bank, need_a=need_a, vfree=vfree):
                    te.wait_ge(C.aS.h, need_a)
                    te.wait_ge(C.vS.h, vfree)
                    for k in range(8):
                        dc = grp * 8 + k
                        i = te.transpose(bank[:, k * 128:(k + 1) * 128],
                                         ybf[:, dc * 128:(dc + 1) * 128], ident[:])
                    C.tS.inc(i)
                C.te(_)
                need_t = C.tS.n
                def _(ve, grp=grp, bank=bank, need_t=need_t, slot=slot):
                    ve.wait_ge(C.tS.h, need_t)
                    dst = yt[:].rearrange("p (dc c) -> p dc c", c=ytile)[
                        :, grp * 8:(grp + 1) * 8, slot * 128:(slot + 1) * 128]
                    i = ve.tensor_copy(out=dst, in_=bank[:].rearrange(
                        "p (k c) -> p k c", c=128))
                    C.vS.inc(i)
                C.ve(_)

        def emit_ln_tile(t, src_ap, xr, sXR, ybf, yt, ytile, sq_scr, psT, slot):
            need_a = ln_tile_stats(t, src_ap, xr, sXR, ybf, sq_scr)
            ln_tile_transpose(need_a, ybf, yt, ytile, psT, slot)

        def emit_ffn(src_ap, dst_ap, tag, with_pass3=False):
            with ExitStack() as es:
                fec = es.enter_context
                ynT = fec(nc.sbuf_tensor(f"{tag}_ynT", [128, NDC * 512], BF16))
                uR = [fec(nc.sbuf_tensor(f"{tag}_u{i}", [128, QC], FP16)) for i in range(2)]
                tP = [fec(nc.sbuf_tensor(f"{tag}_t{i}", [128, QC], F32)) for i in range(2)]
                k16 = [fec(nc.sbuf_tensor(f"{tag}_k{i}", [128, QC], mybir.dt.int16)) for i in range(2)]
                aB = [fec(nc.sbuf_tensor(f"{tag}_a{i}", [128, QC], BF16)) for i in range(4)]
                q1R = [fec(nc.sbuf_tensor(f"{tag}_q1{i}", [128, 4096], BF16)) for i in range(4)]
                fTR = [fec(nc.sbuf_tensor(f"{tag}_fT{i}", [128, 512], BF16)) for i in range(4)]
                ftok = [fec(nc.sbuf_tensor(f"{tag}_ftok{i}", [128, D], BF16)) for i in range(1)]
                xr = fec(nc.sbuf_tensor(f"{tag}_xr", [128, D], F32))
                xr2 = xr
                xo = fec(nc.sbuf_tensor(f"{tag}_xo", [128, D], F32))
                ybf = fec(nc.sbuf_tensor(f"{tag}_ybf", [128, D], BF16))
                utile = [fec(nc.sbuf_tensor(f"{tag}_ut{i}", [128, 512], FP16)) for i in range(4)]
                psA = [fec(nc.psum_tensor(f"{tag}_psA{i}", [128, 512], F32)) for i in range(4)]
                psB = [fec(nc.psum_tensor(f"{tag}_psB{i}", [128, 512], F32)) for i in range(2)]
                psT = [fec(nc.psum_tensor(f"{tag}_psT{i}", [128, 1024], BF16)) for i in range(2)]

                late = {}
                sQ1 = [C.dsem(f"sQ1_{i}") for i in range(4)]
                sQ2 = [C.dsem(f"sQ2_{i}") for i in range(4)]
                sQ2S = [C.dsem(f"sQ2S{i}") for i in range(2)]
                sUT = [C.dsem(f"sUT{i}") for i in range(4)]
                sUR = [C.dsem(f"sUR{i}") for i in range(2)]
                sAB = [C.dsem(f"sAB{i}") for i in range(4)]
                sAT = C.dsem("sAT")
                sFS = [C.dsem(f"sFS{i}") for i in range(4)]
                sFTk = [C.dsem(f"sFTk{i}") for i in range(1)]
                sXR = C.dsem("sXR")
                sXO = C.dsem("sXO")

                st = {
                    "q1_last": [0] * 4,
                    "q2_last": [0] * 4,
                    "ln_v": {},
                    "u_done": {},
                    "a_done": {},
                    "f_done": {},
                    "mm2_done": 0,
                    "u_sq": [0, 0],
                    "dve_p": [0, 0],
                }

                def wv(w, k):
                    return (w % 2) * 32 + k

                p3 = {}
                if with_pass3:
                    p3es = ExitStack()
                    p3 = {
                        "es": p3es,
                        "wb": [p3es.enter_context(
                            nc.sbuf_tensor(f"{tag}_p3w{i}", [128, 4096], F32))
                            for i in range(2)],
                        "qt": [p3es.enter_context(
                            nc.sbuf_tensor(f"{tag}_p3q{i}", [128, 4096], BF16))
                            for i in range(2)],
                        "st": [p3es.enter_context(
                            nc.sbuf_tensor(f"{tag}_p3s{i}", [128, 4096], BF16))
                            for i in range(2)],
                        "nt": p3es.enter_context(
                            nc.sbuf_tensor(f"{tag}_p3n", [128, 4096], BF16)),
                        "sW": [C.dsem(f"p3W{i}") for i in range(2)],
                        "tern_done": [0, 0],
                        "st_done": [0, 0],
                    }

                def emit_pass3_half(h2):
                    dc, half = divmod(h2, 2)
                    sl = h2 % 2
                    buf, sW = p3["wb"][sl], p3["sW"][sl]
                    qt, stg = p3["qt"][sl], p3["st"][sl]
                    vfree = C.vS.n
                    def _(sy, buf=buf, dc=dc, half=half, vfree=vfree, sW=sW):
                        sy.wait_ge(C.vS.h, vfree)
                        sy.dma_start(out=buf[:],
                                     in_=cpj_h[dc * 128:(dc + 1) * 128,
                                               half * 4096:(half + 1) * 4096]
                                     ).then_inc(sW.h, 16)
                    C.sy(_)
                    sW.n += 16
                    need_d = sW.n
                    tfree = p3["tern_done"][sl]
                    def _(ve, buf=buf, need_d=need_d, tfree=tfree, sW=sW, qt=qt):
                        ve.wait_ge(sW.h, need_d)
                        ve.wait_ge(C.tS.h, tfree)
                        i = ve.tensor_scalar(out=p3["nt"][:], in0=buf[:],
                                             scalar1=svec[:, 5:6], scalar2=None,
                                             op0=OP.is_lt)
                        ve.drain()
                        i = ve.scalar_tensor_tensor(out=qt[:], in0=buf[:],
                                                    scalar=svec[:, 4:5], in1=p3["nt"][:],
                                                    op0=OP.is_gt, op1=OP.subtract)
                        C.vS.inc(i)
                    C.ve(_)
                    need_v = C.vS.n
                    st_done = p3["st_done"][sl]
                    sQs2 = sQ2S[sl]
                    for grp in range(4):
                        bank = psT[grp % 2]
                        pvfree = C.vS.n
                        def _(te, grp=grp, bank=bank, need_v=need_v, qt=qt,
                              pvfree=pvfree):
                            te.wait_ge(C.vS.h, max(need_v, pvfree))
                            for k in range(8):
                                jc = grp * 8 + k
                                i = te.transpose(bank[:, k * 128:(k + 1) * 128],
                                                 qt[:, jc * 128:(jc + 1) * 128],
                                                 ident[:])
                            C.tS.inc(i)
                        C.te(_)
                        need_t = C.tS.n
                        def _(ve, grp=grp, bank=bank, need_t=need_t, stg=stg,
                              st_done=st_done, sQs2=sQs2):
                            ve.wait_ge(C.tS.h, need_t)
                            ve.wait_ge(sQs2.h, st_done)
                            i = ve.tensor_copy(out=stg[:, grp * 1024:(grp + 1) * 1024],
                                               in_=bank[:])
                            C.vS.inc(i)
                        C.ve(_)
                    p3["tern_done"][sl] = C.tS.n
                    need_v2 = C.vS.n
                    def _(gp, dc=dc, half=half, need_v2=need_v2, stg=stg, sQs2=sQs2):
                        gp.wait_ge(C.vS.h, need_v2)
                        gp.dma_start(out=q2d[dc, :, half * 4096:(half + 1) * 4096],
                                     in_=stg[:]).then_inc(sQs2.h, 16)
                    C.gp(_)
                    sQs2.n += 16
                    p3["st_done"][sl] = sQs2.n
                    st.setdefault("q2d_ready", {})[(dc, half)] = (sQs2.h, sQs2.n)

                def emit_mm1_loads(jt):
                    for half in range(2):
                        slot = (jt * 2 + half) % 4
                        buf, sB = q1R[slot], sQ1[slot]
                        last = st["q1_last"][slot]
                        def _(se, buf=buf, jt=jt, half=half, last=last, sB=sB):
                            se.wait_ge(C.tS.h, last)
                            se.dma_start(
                                out=buf[:],
                                in_=q1d[jt, :, half * 4096:(half + 1) * 4096],
                            ).then_inc(sB.h, 16)
                        C.se(_)
                        sB.n += 16

                def emit_mm1_jt(w, jt):
                    ln_v = st["ln_v"][w]
                    needs = [(sQ1[(jt * 2 + h) % 4].h, sQ1[(jt * 2 + h) % 4].n)
                             for h in range(2)]
                    vfree = st.get("psA_free", 0)
                    def _(te, jt=jt, needs=needs, vfree=vfree, ln_v=ln_v):
                        for h_, n_ in needs:
                            te.wait_ge(h_, n_)
                        te.wait_ge(C.vS.h, max(vfree, ln_v))
                        for half in range(2):
                            buf = q1R[(jt * 2 + half) % 4]
                            for s4 in range(NS4):
                                for k in range(8):
                                    dc = half * 8 + k
                                    i = te.matmul(
                                        psA[s4][:],
                                        ynT[:, dc * 512 + s4 * 128:
                                            dc * 512 + (s4 + 1) * 128],
                                        buf[:, k * 512:(k + 1) * 512],
                                        start=(dc == 0), stop=(dc == NDC - 1))
                        C.tS.inc(i)
                    C.te(_)
                    st["q1_last"][(jt * 2) % 4] = C.tS.n
                    st["q1_last"][(jt * 2 + 1) % 4] = C.tS.n
                    need_t = C.tS.n
                    for s4 in range(NS4):
                        ut, sU = utile[s4], sUT[s4]
                        stdone = sU.n
                        def _(ve, s4=s4, jt=jt, w=w, ut=ut, need_t=need_t,
                              sU=sU, stdone=stdone):
                            ve.wait_ge(C.tS.h, need_t)
                            ve.wait_ge(sU.h, stdone)
                            i = ve.tensor_scalar(
                                out=ut[:], in0=psA[s4][:],
                                scalar1=0.0, scalar2=None,
                                op0=OP.max, op1=OP.max,
                                accum_out=mparts[:, (w % 2) * 64 + s4 * NJT + jt:
                                                 (w % 2) * 64 + s4 * NJT + jt + 1])
                            C.vS.inc(i)
                        C.ve(_)
                        need_v = C.vS.n
                        def _(gp, s4=s4, jt=jt, w=w, ut=ut, need_v=need_v, sU=sU):
                            gp.wait_ge(C.vS.h, need_v)
                            gp.dma_start(out=u_d[w * NS4 + s4, :,
                                                 jt * 512:(jt + 1) * 512],
                                         in_=ut[:]).then_inc(sU.h, 16)
                        C.gp(_)
                        sU.n += 16
                    st["psA_free"] = C.vS.n

                def emit_scalars_w(w):
                    vprev = C.vS.n
                    def _(ve, w=w, vprev=vprev):
                        ve.wait_ge(C.vS.h, vprev)
                        for s4 in range(NS4):
                            i = ve.tensor_reduce(
                                out=wvec[:, wv(w, s4):wv(w, s4) + 1],
                                in_=mparts[:, (w % 2) * 64 + s4 * NJT:
                                           (w % 2) * 64 + (s4 + 1) * NJT],
                                axis=AX.X, op=OP.max)
                            ve.drain()
                            i = ve.tensor_mul(out=wvec[:, wv(w, 4 + s4):wv(w, 5 + s4)],
                                              in0=wvec[:, wv(w, s4):wv(w, s4) + 1],
                                              in1=wvec[:, wv(w, s4):wv(w, s4) + 1])
                            ve.drain()
                            i = ve.tensor_max(out=wvec[:, wv(w, 4 + s4):wv(w, 5 + s4)],
                                              in0=wvec[:, wv(w, 4 + s4):wv(w, 5 + s4)],
                                              in1=svec[:, 2:3])
                            ve.drain()
                            i = ve.reciprocal(out=wvec[:, wv(w, 8 + s4):wv(w, 9 + s4)],
                                              in_=wvec[:, wv(w, 4 + s4):wv(w, 5 + s4)])
                            ve.drain()
                            i = ve.tensor_scalar(out=wvec[:, wv(w, 8 + s4):wv(w, 9 + s4)],
                                                 in0=wvec[:, wv(w, 8 + s4):wv(w, 9 + s4)],
                                                 scalar1=255.0, scalar2=None,
                                                 op0=OP.mult)
                            i = ve.tensor_mul(out=wvec[:, wv(w, 16 + s4):wv(w, 17 + s4)],
                                              in0=wvec[:, wv(w, 4 + s4):wv(w, 5 + s4)],
                                              in1=svec[:, 3:4])
                            ve.drain()
                        C.vS.inc(i)
                    C.ve(_)
                    need_v = C.vS.n
                    def _(se, w=w, need_v=need_v):
                        se.wait_ge(C.vS.h, need_v)
                        for s4 in range(NS4):
                            i = se.activation(out=wvec[:, wv(w, 12 + s4):wv(w, 13 + s4)],
                                              in_=wvec[:, wv(w, 8 + s4):wv(w, 9 + s4)],
                                              func=AF.Sqrt, bias=0.0, scale=1.0)
                        C.aS.inc(i)
                    C.se(_)
                    need_a = C.aS.n
                    def _(ve, w=w, need_a=need_a):
                        ve.wait_ge(C.aS.h, need_a)
                        for s4 in range(NS4):
                            i = ve.reciprocal(out=wvec[:, wv(w, 20 + s4):wv(w, 21 + s4)],
                                              in_=wvec[:, wv(w, 12 + s4):wv(w, 13 + s4)])
                            ve.drain()
                            i = ve.tensor_mul(out=wvec[:, wv(w, 20 + s4):wv(w, 21 + s4)],
                                              in0=wvec[:, wv(w, 20 + s4):wv(w, 21 + s4)],
                                              in1=wvec[:, wv(w, 8 + s4):wv(w, 9 + s4)])
                            ve.drain()
                            i = ve.tensor_add(out=wvec[:, wv(w, 20 + s4):wv(w, 21 + s4)],
                                              in0=wvec[:, wv(w, 20 + s4):wv(w, 21 + s4)],
                                              in1=wvec[:, wv(w, 12 + s4):wv(w, 13 + s4)])
                            ve.drain()
                            i = ve.tensor_scalar(out=wvec[:, wv(w, 12 + s4):wv(w, 13 + s4)],
                                                 in0=wvec[:, wv(w, 20 + s4):wv(w, 21 + s4)],
                                                 scalar1=0.5, scalar2=None,
                                                 op0=OP.mult)
                            ve.drain()
                        C.vS.inc(i)
                    C.ve(_)
                    return C.vS.n

                def emit_qrelu2_chunk(w, gchk, sqc_ready):
                    u_done = st["u_done"][w]
                    chk, s4 = divmod(gchk, NS4)
                    p = gchk % 2
                    ub, sU = uR[p], sUR[p]
                    aprev = st["u_sq"][p]
                    def _(se, s4=s4, w=w, chk=chk, ub=ub, sU=sU,
                          u_done=u_done, aprev=aprev):
                        for h_, n_ in u_done:
                            se.wait_ge(h_, n_)
                        se.wait_ge(C.aS.h, aprev)
                        se.dma_start(out=ub[:],
                                     in_=u_d[w * NS4 + s4, :,
                                             chk * QC:(chk + 1) * QC]
                                     ).then_inc(sU.h, 16)
                    C.se(_)
                    sU.n += 16
                    need_d = sU.n
                    vprev = st["dve_p"][p]
                    def _(se, s4=s4, w=w, ub=ub, sU=sU, need_d=need_d,
                          sqc_ready=sqc_ready, vprev=vprev, p=p):
                        se.wait_ge(sU.h, need_d)
                        se.wait_ge(C.vS.h, max(vprev, sqc_ready))
                        i = se.activation(out=tP[p][:], in_=ub[:],
                                          func=AF.Square, bias=0.0,
                                          scale=wvec[:, wv(w, 12 + s4):
                                                     wv(w, 13 + s4)])
                        C.aS.inc(i)
                    C.se(_)
                    need_a = C.aS.n
                    st["u_sq"][p] = need_a
                    ab, sA = aB[gchk % 4], sAB[gchk % 4]
                    stdone = sA.n
                    def _(ve, need_a=need_a, ab=ab, sA=sA, stdone=stdone, p=p):
                        ve.wait_ge(C.aS.h, need_a)
                        ve.wait_ge(sA.h, stdone)
                        i = ve.tensor_copy(out=k16[p][:], in_=tP[p][:])
                        ve.drain()
                        i = ve.scalar_tensor_tensor(out=tP[p][:], in0=k16[p][:],
                                                    scalar=-1.0, in1=tP[p][:],
                                                    op0=OP.mult, op1=OP.add)
                        ve.drain()
                        i = ve.scalar_tensor_tensor(out=ab[:], in0=tP[p][:],
                                                    scalar=0.5, in1=k16[p][:],
                                                    op0=OP.is_ge, op1=OP.add)
                        C.vS.inc(i)
                    C.ve(_)
                    need_v = C.vS.n
                    st["dve_p"][p] = need_v
                    def _(gp, s4=s4, w=w, chk=chk, ab=ab, sA=sA, need_v=need_v):
                        gp.wait_ge(C.vS.h, need_v)
                        t0 = (w * NS4 + s4) * 128
                        gp.dma_start(out=a_d[t0:t0 + 128,
                                             chk * QC:(chk + 1) * QC],
                                     in_=ab[:]).then_inc(sA.h, 16)
                    C.gp(_)
                    sA.n += 16

                def emit_at_group(w, cgrp, extra=()):
                    a_done = [(sem.h, sem.n) for sem in sAB] + list(extra)
                    mm2_prev = st["mm2_done"]
                    aT = late["aT"]
                    for k in range(8):
                        jc = cgrp * 8 + k
                        def _(sy, jc=jc, k=k, w=w, a_done=a_done, mm2_prev=mm2_prev,
                              aT=aT):
                            if k == 0:
                                for h_, n_ in a_done:
                                    sy.wait_ge(h_, n_)
                                sy.wait_ge(C.tS.h, mm2_prev)
                            sy.dma_start(
                                out=aT[:, jc * 512:(jc + 1) * 512],
                                in_=a_d[w * SW:(w + 1) * SW, jc * 128:(jc + 1) * 128],
                                transpose=True).then_inc(sAT.h, 16)
                        C.sy(_)
                        sAT.n += 16

                def emit_mm2_loads(dblk):
                    for half in range(2):
                        slot = (dblk * 2 + half) % 4
                        buf, sB = late["q2R"][slot], sQ2[slot]
                        last = st["q2_last"][slot]
                        rdy = st.get("q2d_ready", {}).get((dblk, half))
                        def _(se, buf=buf, dblk=dblk, half=half, last=last, sB=sB,
                              rdy=rdy):
                            se.wait_ge(C.tS.h, last)
                            if rdy is not None:
                                se.wait_ge(*rdy)
                            se.dma_start(out=buf[:],
                                         in_=q2d[dblk, :,
                                                 half * 4096:(half + 1) * 4096]
                                         ).then_inc(sB.h, 16)
                        C.se(_)
                        sB.n += 16

                def emit_mm2_dblk(w, dblk, aT_ready):
                    needs = [(sQ2[(dblk * 2 + h) % 4].h, sQ2[(dblk * 2 + h) % 4].n)
                             for h in range(2)]
                    bank = psB[dblk % 2]
                    vfree = st.get("psB_free", 0)
                    aT = late["aT"]
                    q2Rl = late["q2R"]
                    def _(te, dblk=dblk, bank=bank, needs=needs,
                          vfree=vfree, aT_ready=aT_ready, aT=aT, q2Rl=q2Rl):
                        for h_, n_ in needs:
                            te.wait_ge(h_, n_)
                        te.wait_ge(sAT.h, aT_ready)
                        te.wait_ge(C.vS.h, vfree)
                        for jc in range(NJC):
                            buf = q2Rl[(dblk * 2 + jc // 32) % 4]
                            i = te.matmul(bank[:],
                                          buf[:, (jc % 32) * 128:
                                              (jc % 32 + 1) * 128],
                                          aT[:, jc * 512:(jc + 1) * 512],
                                          start=(jc == 0), stop=(jc == NJC - 1))
                        C.tS.inc(i)
                    C.te(_)
                    st["q2_last"][(dblk * 2) % 4] = C.tS.n
                    st["q2_last"][(dblk * 2 + 1) % 4] = C.tS.n
                    need_t = C.tS.n
                    fsl = dblk % 4
                    fslot, sF = fTR[fsl], sFS[fsl]
                    stdone = sF.n
                    def _(ve, fslot=fslot, need_t=need_t, bank=bank,
                          sF=sF, stdone=stdone):
                        ve.wait_ge(C.tS.h, need_t)
                        ve.wait_ge(sF.h, stdone)
                        i = ve.tensor_copy(out=fslot[:], in_=bank[:])
                        C.vS.inc(i)
                    C.ve(_)
                    st["psB_free"] = C.vS.n
                    need_v = C.vS.n
                    def _(gp, dblk=dblk, w=w, fslot=fslot, need_v=need_v, sF=sF):
                        gp.wait_ge(C.vS.h, need_v)
                        gp.dma_start(out=f_d[dblk * 128:(dblk + 1) * 128,
                                             w * SW:(w + 1) * SW],
                                     in_=fslot[:]).then_inc(sF.h, 16)
                    C.gp(_)
                    sF.n += 16

                def emit_residual_s4(w, s4):
                    f_done = st["f_done"][w]
                    t = w * NS4 + s4
                    fslot, sFk = ftok[0], sFTk[0]
                    vfree = C.vS.n
                    def _(sy, t=t, fslot=fslot, f_done=f_done, vfree=vfree,
                          sFk=sFk):
                        for h_, n_ in f_done:
                            sy.wait_ge(h_, n_)
                        sy.wait_ge(C.vS.h, vfree)
                        sy.dma_start(out=fslot[:],
                                     in_=f_d[:, t * 128:(t + 1) * 128],
                                     transpose=True).then_inc(sFk.h, 16)
                    C.sy(_)
                    sFk.n += 16
                    need_fk = sFk.n
                    aprev3 = C.aS.n
                    def _(se, t=t, vfree=vfree, aprev3=aprev3):
                        se.wait_ge(C.vS.h, vfree)
                        se.wait_ge(C.aS.h, aprev3)
                        se.dma_start(out=xr2[:],
                                     in_=src_ap[t * 128:(t + 1) * 128, :]
                                     ).then_inc(sXR.h, 16)
                    C.se(_)
                    sXR.n += 16
                    need_xr = sXR.n
                    xodone = sXO.n
                    def _(ve, w=w, s4=s4, fslot=fslot, sFk=sFk, need_fk=need_fk,
                          need_xr=need_xr, xodone=xodone):
                        ve.wait_ge(sFk.h, need_fk)
                        ve.wait_ge(sXR.h, need_xr)
                        ve.wait_ge(sXO.h, xodone)
                        i = ve.scalar_tensor_tensor(
                            out=xo[:], in0=fslot[:],
                            scalar=wvec[:, wv(w, 16 + s4):wv(w, 17 + s4)],
                            in1=xr2[:], op0=OP.mult, op1=OP.add)
                        C.vS.inc(i)
                    C.ve(_)
                    need_v = C.vS.n
                    def _(gp, t=t, need_v=need_v):
                        gp.wait_ge(C.vS.h, need_v)
                        gp.dma_start(out=dst_ap[t * 128:(t + 1) * 128, :],
                                     in_=xo[:]).then_inc(sXO.h, 16)
                    C.gp(_)
                    sXO.n += 16

                def emit_ln_w(w):
                    for s4 in range(NS4):
                        emit_ln_tile(w * NS4 + s4, src_ap, xr, sXR, ybf, ynT, 512,
                                     ybf[:, 0:QC], psT, s4)
                    st["ln_v"][w] = C.vS.n

                emit_ln_w(0)
                emit_mm1_loads(0)
                for jt in range(NJT):
                    if jt + 1 < NJT:
                        emit_mm1_loads(jt + 1)
                    emit_mm1_jt(0, jt)
                    if with_pass3:
                        emit_pass3_half(2 * jt)
                        emit_pass3_half(2 * jt + 1)
                st["u_done"][0] = [(sem.h, sem.n) for sem in sUT]
                emit_ln_w(1)
                for K in range(NW):
                    sqc_ready = emit_scalars_w(K)
                    if K == 0:
                        if K + 1 < NW:
                            emit_mm1_loads(0)
                            for jt in range(NJT):
                                if jt + 1 < NJT:
                                    emit_mm1_loads(jt + 1)
                                emit_mm1_jt(K + 1, jt)
                                emit_qrelu2_chunk(K, 2 * jt, sqc_ready)
                                emit_qrelu2_chunk(K, 2 * jt + 1, sqc_ready)
                            st["u_done"][K + 1] = [(sem.h, sem.n) for sem in sUT]
                        else:
                            for g in range(NS4 * NQC):
                                emit_qrelu2_chunk(K, g, sqc_ready)
                        p3done = []
                        if with_pass3:
                            p3["es"].close()
                            p3done = [(sem.h, sem.n) for sem in sQ2S]
                        late["aT"] = fec(nc.sbuf_tensor(f"{tag}_aT",
                                                        [128, NJC * 512], BF16))
                        late["q2R"] = [fec(nc.sbuf_tensor(f"{tag}_q2{i}",
                                                          [128, 4096], BF16))
                                       for i in range(4)]
                        for cgrp in range(8):
                            emit_at_group(K, cgrp, extra=p3done)
                    elif K + 1 < NW:
                        emit_mm1_loads(0)
                        for jt in range(NJT):
                            if jt + 1 < NJT:
                                emit_mm1_loads(jt + 1)
                            emit_mm1_jt(K + 1, jt)
                            emit_qrelu2_chunk(K, 2 * jt, sqc_ready)
                            emit_qrelu2_chunk(K, 2 * jt + 1, sqc_ready)
                            if jt % 2 == 1:
                                emit_at_group(K, jt // 2)
                        st["u_done"][K + 1] = [(sem.h, sem.n) for sem in sUT]
                    else:
                        for g in range(NS4 * NQC):
                            emit_qrelu2_chunk(K, g, sqc_ready)
                            if g % 4 == 3:
                                emit_at_group(K, g // 4)
                    aT_ready = sAT.n
                    ln_w = K + 2 if K + 2 < NW else None
                    res_w = K - 1 if K >= 1 else None
                    ln_needa = []
                    emit_mm2_loads(0)
                    for dblk in range(NDC):
                        if dblk + 1 < NDC:
                            emit_mm2_loads(dblk + 1)
                        emit_mm2_dblk(K, dblk, aT_ready)
                        if dblk % 4 == 1 and ln_w is not None:
                            ln_needa.append(ln_tile_stats(
                                ln_w * NS4 + dblk // 4, src_ap, xr, sXR, ybf,
                                ybf[:, 0:QC]))
                        if dblk % 4 == 2 and res_w is not None:
                            emit_residual_s4(res_w, dblk // 4)
                        if dblk % 4 == 3 and ln_w is not None:
                            ln_tile_transpose(ln_needa[dblk // 4], ybf, ynT, 512,
                                              psT, dblk // 4)
                    st["mm2_done"] = C.tS.n
                    st["f_done"][K] = [(sem.h, sem.n) for sem in sFS]
                    if ln_w is not None:
                        st["ln_v"][ln_w] = C.vS.n
                for s4 in range(NS4):
                    emit_residual_s4(NW - 1, s4)
                C.barrier()

        def emit_attn():
            with ExitStack() as esb:
                bec = esb.enter_context
                xr = bec(nc.sbuf_tensor("b_xr", [128, D], F32))
                xo = bec(nc.sbuf_tensor("b_xo", [128, D], F32))
                ybf = bec(nc.sbuf_tensor("b_ybf", [128, D], BF16))
                sXR = C.dsem("sXR")
                sXO = C.dsem("sXO")
                sWS = [C.dsem(f"sWS{i}") for i in range(2)]
                sQT = [C.dsem(f"sQT{i}") for i in range(2)]

                with ExitStack() as esvk:
                    vkec = esvk.enter_context
                    vT = vkec(nc.sbuf_tensor("b_vT", [128, NDC * 2048], BF16))

                    with ExitStack() as esy:
                        yec = esy.enter_context
                        y1T = yec(nc.sbuf_tensor("b_y1T", [128, NDC * 2048], BF16,
                                                 side="right"))
                        wbB = [yec(nc.sbuf_tensor(f"b_wb{i}", [128, NDC * 512], BF16))
                               for i in range(2)]
                        wtmp = [yec(nc.sbuf_tensor(f"b_wt{i}", [128, 512], F32))
                                for i in range(2)]
                        qt_ = [yec(nc.sbuf_tensor(f"b_qt{i}", [128, 512], BF16))
                               for i in range(2)]
                        psT = [yec(nc.psum_tensor(f"b_psT{i}", [128, 1024], BF16))
                               for i in range(2)]
                        psQ = [yec(nc.psum_tensor(f"b_psQ{i}", [128, 512], F32))
                               for i in range(4)]

                        wb_mm_done = [0, 0]
                        cp_done = [0, 0]
                        quarters = ([(wq_h, et, 'q') for et in range(4)]
                                    + [(wk_h, et, 'k') for et in range(4)]
                                    + [(wv_h, et, 'v') for et in range(4)])
                        NQ = len(quarters)
                        slab_v = [0] * NQ

                        def emit_load_step(qi, j):
                            w_h_, et, _k = quarters[qi]
                            slot = qi % 2
                            wb = wbB[slot]
                            tsl = j % 2
                            tmp, sW = wtmp[tsl], sWS[tsl]
                            cfree = cp_done[tsl]
                            def _(sy, tmp=tmp, j=j, et=et, w_h_=w_h_,
                                  cfree=cfree, sW=sW):
                                sy.wait_ge(C.vS.h, cfree)
                                sy.dma_start(
                                    out=tmp[:],
                                    in_=w_h_[j * 128:(j + 1) * 128,
                                             et * 512:(et + 1) * 512]
                                ).then_inc(sW.h, 16)
                            C.sy(_)
                            sW.n += 16
                            need_d = sW.n
                            tfree = wb_mm_done[slot]
                            def _(ve, tmp=tmp, j=j, need_d=need_d,
                                  tfree=tfree, sW=sW, wb=wb):
                                ve.wait_ge(sW.h, need_d)
                                ve.wait_ge(C.tS.h, tfree)
                                i = ve.tensor_copy(
                                    out=wb[:, j * 512:(j + 1) * 512],
                                    in_=tmp[:])
                                C.vS.inc(i)
                            C.ve(_)
                            cp_done[tsl] = C.vS.n
                            if j == NDC - 1:
                                slab_v[qi] = C.vS.n

                        for j in range(NDC):
                            emit_load_step(0, j)

                        ln_done = [0] * NT
                        evac_done = [0, 0, 0, 0]
                        G = 0
                        for qi, (w_h_, et, kind) in enumerate(quarters):
                            slot = qi % 2
                            wb = wbB[slot]
                            for j in range(NT):
                                if qi + 1 < NQ:
                                    emit_load_step(qi + 1, j)
                                if qi == 0:
                                    emit_ln_tile(j, x1d, xr, sXR, ybf, y1T,
                                                 2048, ybf[:, 0:QC], psT, j)
                                    ln_done[j] = C.vS.n
                                bank = psQ[G % 4]
                                needs = max(slab_v[qi], evac_done[G % 4],
                                            ln_done[j if qi == 0 else NT - 1])
                                if kind in ('q', 'k'):
                                    t = j
                                    def _(te, t=t, bank=bank, wb=wb, needs=needs):
                                        te.wait_ge(C.vS.h, needs)
                                        for dc in range(NDC):
                                            i = te.matmul(
                                                bank[:],
                                                y1T[:, dc * 2048 + t * 128:
                                                    dc * 2048 + (t + 1) * 128],
                                                wb[:, dc * 512:(dc + 1) * 512],
                                                start=(dc == 0),
                                                stop=(dc == NDC - 1))
                                        C.tS.inc(i)
                                    C.te(_)
                                else:
                                    fq, sw = divmod(j, 4)
                                    def _(te, fq=fq, sw=sw, bank=bank, wb=wb,
                                          needs=needs):
                                        te.wait_ge(C.vS.h, needs)
                                        for dc in range(NDC):
                                            i = te.matmul(
                                                bank[:],
                                                wb[:, dc * 512 + fq * 128:
                                                   dc * 512 + (fq + 1) * 128],
                                                y1T[:, dc * 2048 + sw * 512:
                                                    dc * 2048 + (sw + 1) * 512],
                                                start=(dc == 0),
                                                stop=(dc == NDC - 1))
                                        C.tS.inc(i)
                                    C.te(_)
                                wb_mm_done[slot] = C.tS.n
                                need_t = C.tS.n
                                if kind == 'v':
                                    ft = et * 4 + fq
                                    def _(ve, ft=ft, sw=sw, bank=bank,
                                          need_t=need_t):
                                        ve.wait_ge(C.tS.h, need_t)
                                        i = ve.tensor_copy(
                                            out=vT[:, ft * 2048 + sw * 512:
                                                   ft * 2048 + (sw + 1) * 512],
                                            in_=bank[:])
                                        C.vS.inc(i)
                                    C.ve(_)
                                else:
                                    t = j
                                    qsl = G % 2
                                    qb, sQs = qt_[qsl], sQT[qsl]
                                    stdone = sQs.n
                                    def _(ve, kind=kind, bank=bank, qb=qb,
                                          need_t=need_t, sQs=sQs, stdone=stdone):
                                        ve.wait_ge(C.tS.h, need_t)
                                        ve.wait_ge(sQs.h, stdone)
                                        if kind == 'q':
                                            i = ve.tensor_scalar(
                                                out=qb[:], in0=bank[:],
                                                scalar1=0.125, scalar2=None,
                                                op0=OP.mult)
                                        else:
                                            i = ve.tensor_copy(out=qb[:],
                                                               in_=bank[:])
                                        C.vS.inc(i)
                                    C.ve(_)
                                    need_v = C.vS.n
                                    def _(gp, kind=kind, t=t, et=et, qb=qb,
                                          need_v=need_v, sQs=sQs):
                                        gp.wait_ge(C.vS.h, need_v)
                                        if kind == 'q':
                                            gp.dma_start(
                                                out=qd[t * 128:(t + 1) * 128,
                                                       et * 512:(et + 1) * 512],
                                                in_=qb[:]
                                            ).then_inc(sQs.h, 16)
                                        else:
                                            gp.dma_start(
                                                out=kd[:, t * 2048 + et * 512:
                                                       t * 2048 + (et + 1) * 512],
                                                in_=qb[:]
                                            ).then_inc(sQs.h, 16)
                                    C.gp(_)
                                    sQs.n += 16
                                evac_done[G % 4] = C.vS.n
                                G += 1
                        qk_done = [(sem.h, sem.n) for sem in sQT]
                        C.barrier()

                    with ExitStack() as eso:
                        oec = eso.enter_context
                        kT = oec(nc.sbuf_tensor("b_kT", [128, NT * 2048], BF16,
                                                side="right"))
                        qcb = [oec(nc.sbuf_tensor(f"b_qc{i}", [128, NT * 128], BF16))
                               for i in range(2)]
                        wsb = oec(nc.sbuf_tensor("b_wsb", [128, 2048], BF16))
                        wTc = [oec(nc.sbuf_tensor(f"b_wTc{i}", [128, 2048], BF16))
                               for i in range(2)]
                        oR = [oec(nc.sbuf_tensor(f"b_oR{i}", [128, 512], BF16))
                              for i in range(2)]
                        psS = oec(nc.psum_tensor("b_psS", [128, 2048], F32))
                        psO = [oec(nc.psum_tensor(f"b_psO{i}", [128, 512], F32))
                               for i in range(2)]
                        psW = oec(nc.psum_tensor("b_psW", [128, 1024], BF16))
                        sKT = C.dsem("sKT")
                        sQCb = [C.dsem(f"sQC{i}") for i in range(2)]
                        sOR = [C.dsem(f"sOR{i}") for i in range(2)]

                        def _(sy, qk_done=qk_done):
                            for h_, n_ in qk_done:
                                sy.wait_ge(h_, n_)
                            sy.dma_start(out=kT[:], in_=kd).then_inc(sKT.h, 16)
                        C.sy(_)
                        sKT.n += 16
                        kT_ld = sKT.n

                        qsc_last = [0, 0]
                        wT_prev = None
                        def emit_omm(e, wT_v, rs_ready, wTcb):
                            for sw in range(4):
                                bank = psO[sw % 2]
                                vfree = C.vS.n
                                def _(te, bank=bank, sw=sw, wT_v=wT_v, vfree=vfree,
                                      wTcb=wTcb):
                                    te.wait_ge(C.vS.h, max(wT_v, vfree))
                                    for fc in range(NT):
                                        i = te.matmul(
                                            bank[:],
                                            wTcb[:, fc * 128:(fc + 1) * 128],
                                            vT[:, fc * 2048 + sw * 512:
                                               fc * 2048 + (sw + 1) * 512],
                                            start=(fc == 0), stop=(fc == NT - 1))
                                    C.tS.inc(i)
                                C.te(_)
                                need_t = C.tS.n
                                osl = sw % 2
                                ob, sO = oR[osl], sOR[osl]
                                stdone = sO.n
                                def _(ve, e=e, ob=ob, bank=bank, need_t=need_t,
                                      rs_ready=rs_ready, sO=sO, stdone=stdone):
                                    ve.wait_ge(C.tS.h, need_t)
                                    ve.wait_ge(C.vS.h, rs_ready)
                                    ve.wait_ge(sO.h, stdone)
                                    i = ve.tensor_scalar(
                                        out=ob[:], in0=bank[:],
                                        scalar1=wvec[:, 24 + (e % 2) * 32 + 2:
                                                     24 + (e % 2) * 32 + 3],
                                        scalar2=None, op0=OP.mult)
                                    C.vS.inc(i)
                                C.ve(_)
                                need_v = C.vS.n
                                def _(gp, e=e, sw=sw, ob=ob, need_v=need_v, sO=sO):
                                    gp.wait_ge(C.vS.h, need_v)
                                    gp.dma_start(
                                        out=od[:, e * 2048 + sw * 512:
                                               e * 2048 + (sw + 1) * 512],
                                        in_=ob[:]).then_inc(sO.h, 16)
                                C.gp(_)
                                sO.n += 16

                        for e in range(NT):
                            wb = 24 + (e % 2) * 32
                            qc = qcb[e % 2]
                            sQC = sQCb[e % 2]
                            tfree = qsc_last[e % 2]
                            def _(sy, e=e, qc=qc, tfree=tfree, sQC=sQC):
                                sy.wait_ge(C.tS.h, tfree)
                                src = qd[:, e * 128:(e + 1) * 128].rearrange(
                                    "(sc p) c -> p sc c", p=128)
                                dst = qc[:].rearrange("p (sc c) -> p sc c", c=128)
                                sy.dma_start(out=dst, in_=src).then_inc(sQC.h, 16)
                            C.sy(_)
                            sQC.n += 16
                            need_d = sQC.n
                            aprev = C.aS.n
                            def _(te, qc=qc, sQC=sQC, need_d=need_d, aprev=aprev,
                                  kT_ld=kT_ld):
                                te.wait_ge(sQC.h, need_d)
                                te.wait_ge(sKT.h, kT_ld)
                                te.wait_ge(C.aS.h, aprev)
                                for sc in range(NT):
                                    for ft in range(4):
                                        i = te.matmul(
                                            psS[:, ft * 512:(ft + 1) * 512],
                                            qc[:, sc * 128:(sc + 1) * 128],
                                            kT[:, sc * 2048 + ft * 512:
                                               sc * 2048 + (ft + 1) * 512],
                                            start=(sc == 0), stop=(sc == NT - 1))
                                C.tS.inc(i)
                            C.te(_)
                            qsc_last[e % 2] = C.tS.n
                            need_t = C.tS.n
                            def _(ve, wb=wb, need_t=need_t):
                                ve.wait_ge(C.tS.h, need_t)
                                i = ve.tensor_reduce(out=wvec[:, wb:wb + 1], in_=psS[:],
                                                     axis=AX.X, op=OP.max, negate=True)
                                C.vS.inc(i)
                            C.ve(_)
                            need_v = C.vS.n
                            tprev = C.tS.n
                            def _(se, wb=wb, need_v=need_v, tprev=tprev):
                                se.wait_ge(C.vS.h, need_v)
                                se.wait_ge(C.tS.h, tprev)
                                i = se.activation(out=wsb[:], in_=psS[:], func=AF.Exp,
                                                  bias=wvec[:, wb:wb + 1], scale=1.0,
                                                  accum_out=wvec[:, wb + 1:wb + 2])
                                C.aS.inc(i)
                            C.se(_)
                            need_a = C.aS.n
                            def _(ve, wb=wb, need_a=need_a):
                                ve.wait_ge(C.aS.h, need_a)
                                i = ve.reciprocal(out=wvec[:, wb + 2:wb + 3],
                                                  in_=wvec[:, wb + 1:wb + 2])
                                C.vS.inc(i)
                            C.ve(_)
                            rs_ready = C.vS.n
                            wTcb = wTc[e % 2]
                            for grp in range(2):
                                vfree = C.vS.n
                                def _(te, grp=grp, need_a=need_a, vfree=vfree):
                                    te.wait_ge(C.aS.h, need_a)
                                    te.wait_ge(C.vS.h, vfree)
                                    for k in range(8):
                                        fc = grp * 8 + k
                                        i = te.transpose(
                                            psW[:, k * 128:(k + 1) * 128],
                                            wsb[:, fc * 128:(fc + 1) * 128],
                                            ident[:])
                                    C.tS.inc(i)
                                C.te(_)
                                need_t = C.tS.n
                                def _(ve, grp=grp, need_t=need_t, wTcb=wTcb):
                                    ve.wait_ge(C.tS.h, need_t)
                                    i = ve.tensor_copy(
                                        out=wTcb[:, grp * 1024:(grp + 1) * 1024],
                                        in_=psW[:])
                                    C.vS.inc(i)
                                C.ve(_)
                            if wT_prev is not None:
                                emit_omm(*wT_prev)
                            wT_prev = (e, C.vS.n, rs_ready, wTcb)
                        emit_omm(*wT_prev)
                        C.barrier()

                with ExitStack() as es3:
                    tec = es3.enter_context
                    o2 = tec(nc.sbuf_tensor("b_o2", [128, NDC * 2048], BF16,
                                            side="right"))
                    wob = tec(nc.sbuf_tensor("b_wob", [128, NDC * 2048], BF16))
                    wtmp = [tec(nc.sbuf_tensor(f"b_wtmp{i}", [128, 2048], F32))
                            for i in range(2)]
                    psP = [tec(nc.psum_tensor(f"b_psP{i}", [128, 512], F32))
                           for i in range(2)]
                    sO2 = C.dsem("sO2")
                    sWT = [C.dsem(f"sWT{i}") for i in range(2)]

                    def _(sy):
                        sy.dma_start(out=o2[:], in_=od).then_inc(sO2.h, 16)
                    C.sy(_)
                    sO2.n += 16
                    o2_ld = sO2.n
                    for ecx in range(NDC):
                        sl = ecx % 2
                        slot, sW = wtmp[sl], sWT[sl]
                        vfree = C.vS.n
                        def _(sy, slot=slot, ecx=ecx, vfree=vfree, sW=sW):
                            sy.wait_ge(C.vS.h, max(0, vfree - 1))
                            sy.dma_start(out=slot[:],
                                         in_=wo_h[ecx * 128:(ecx + 1) * 128, :]
                                         ).then_inc(sW.h, 16)
                        C.sy(_)
                        sW.n += 16
                        need_d = sW.n
                        def _(ve, slot=slot, ecx=ecx, need_d=need_d, sW=sW):
                            ve.wait_ge(sW.h, need_d)
                            i = ve.tensor_copy(out=wob[:, ecx * 2048:(ecx + 1) * 2048],
                                               in_=slot[:])
                            C.vS.inc(i)
                        C.ve(_)
                    wo_v = C.vS.n

                    for t in range(NT):
                        vfree = C.vS.n
                        aprev = C.aS.n
                        def _(se, t=t, vfree=vfree, aprev=aprev):
                            se.wait_ge(C.vS.h, vfree)
                            se.wait_ge(C.aS.h, aprev)
                            se.dma_start(out=xr[:], in_=x1d[t * 128:(t + 1) * 128, :]
                                         ).then_inc(sXR.h, 16)
                        C.se(_)
                        sXR.n += 16
                        need_xr = sXR.n
                        for gt in range(4):
                            bank = psP[gt % 2]
                            vfree = C.vS.n
                            def _(te, t=t, gt=gt, bank=bank, wo_v=wo_v,
                                  vfree=vfree, o2_ld=o2_ld, sO2=sO2):
                                te.wait_ge(C.vS.h, max(wo_v, vfree))
                                te.wait_ge(sO2.h, o2_ld)
                                for ecx in range(NDC):
                                    i = te.matmul(
                                        bank[:],
                                        o2[:, ecx * 2048 + t * 128:
                                           ecx * 2048 + (t + 1) * 128],
                                        wob[:, ecx * 2048 + gt * 512:
                                            ecx * 2048 + (gt + 1) * 512],
                                        start=(ecx == 0), stop=(ecx == NDC - 1))
                                C.tS.inc(i)
                            C.te(_)
                            need_t = C.tS.n
                            xodone = sXO.n
                            def _(ve, gt=gt, bank=bank, need_t=need_t,
                                  need_xr=need_xr, xodone=xodone):
                                ve.wait_ge(C.tS.h, need_t)
                                ve.wait_ge(sXR.h, need_xr)
                                ve.wait_ge(sXO.h, xodone)
                                i = ve.tensor_add(out=xo[:, gt * 512:(gt + 1) * 512],
                                                  in0=bank[:],
                                                  in1=xr[:, gt * 512:(gt + 1) * 512])
                                C.vS.inc(i)
                            C.ve(_)
                        need_v = C.vS.n
                        def _(gp, t=t, need_v=need_v):
                            gp.wait_ge(C.vS.h, need_v)
                            gp.dma_start(out=x2d[t * 128:(t + 1) * 128, :],
                                         in_=xo[:]).then_inc(sXO.h, 16)
                        C.gp(_)
                        sXO.n += 16
                    C.barrier()

        emit_prep()
        emit_ffn(x_h, x1d, "A", with_pass3=True)
        emit_attn()
        emit_ffn(x2d, out_h, "Cc")

        snap = [(s.h, s.n) for s in C.all_sems if s.n]
        def _(sy, snap=snap):
            for h, n in snap:
                sy.wait_ge(h, n)
        C.sy(_)

    return nc


def kernel(L0_attn_x, L0_attn_Wq, L0_attn_Wk, L0_attn_Wv, L0_attn_Wo,
           L0_ffn_c_fc, L0_ffn_c_proj):
    bf16 = ml_dtypes.bfloat16
    x = np.ascontiguousarray(np.asarray(L0_attn_x, dtype=np.float32))
    ident = np.eye(128, dtype=bf16)
    ones = np.ones((128, 128), dtype=np.float32)
    common = {
        "Wq": np.ascontiguousarray(np.asarray(L0_attn_Wq, dtype=np.float32)),
        "Wk": np.ascontiguousarray(np.asarray(L0_attn_Wk, dtype=np.float32)),
        "Wv": np.ascontiguousarray(np.asarray(L0_attn_Wv, dtype=np.float32)),
        "Wo": np.ascontiguousarray(np.asarray(L0_attn_Wo, dtype=np.float32)),
        "cfc": np.ascontiguousarray(np.asarray(L0_ffn_c_fc, dtype=np.float32)),
        "cpj": np.ascontiguousarray(np.asarray(L0_ffn_c_proj, dtype=np.float32)),
        "identin": ident, "onesin": ones,
    }
    in_maps = [{"x": x[b], **common} for b in range(8)]
    nc = build_program()
    res = run_bass_kernel_spmd(nc, in_maps, core_ids=list(range(8)))
    return np.stack([np.asarray(res.results[b]["out"], dtype=np.float32)
                     for b in range(8)], axis=0)
